# version 28
# baseline (speedup 1.0000x reference)
import numpy as np
from contextlib import ExitStack

import concourse.bass as bass
import concourse.mybir as mybir
from concourse.bass_utils import run_bass_kernel_spmd

F32 = mybir.dt.float32
BF16 = mybir.dt.bfloat16
F16 = mybir.dt.float16
A = mybir.AluOpType
AF = mybir.ActivationFunctionType
AX = mybir.AxisListType

D = 1024
E3 = 3072
ESW = 2048
DFF = 2816
KD = D // 128
KF = DFF // 128
LN_EPS = 1e-5
RMS_EPS = 1.1920929e-07
W_EPS = 1e-6

ENGS = ("sync", "scalar", "vector", "tensor", "gpsimd")
S, Y, V, Tn, G = ENGS


class Ctr:
    def __init__(self, nc, es, name, step=1):
        self.sem = es.enter_context(nc.semaphore(name))
        self.n = 0
        self.step = step

    def bump(self, k=1):
        self.n += k * self.step
        return self.n


class Plan:
    def __init__(self, nc):
        self.nc = nc
        self.q = {e: [] for e in ENGS}
        self.last_wait = {e: {} for e in ENGS}

    def step(self, eng, mk, waits=(), incs=()):
        wl = []
        for c, v in waits:
            val = c.n if v is None else v
            if val <= 0:
                continue
            if self.last_wait[eng].get(c.sem.name, -1) >= val:
                continue
            self.last_wait[eng][c.sem.name] = val
            wl.append((c.sem, val))
        il = []
        for c, k in incs:
            il.append((c.sem, k * c.step))
            c.bump(k)

        def f(e, wl=wl, il=il, mk=mk):
            for s_, v_ in wl:
                e.wait_ge(s_, v_)
            inst = mk(e)
            for s_, v_ in il:
                inst.then_inc(s_, v_)
        self.q[eng].append(f)


def build(TT=512, NT=8):
    T = TT * NT
    NJ = TT // 128
    nc = bass.Bass("TRN2", target_bir_lowering=False)

    x_h = nc.dram_tensor("x", [T, D], F32, kind="ExternalInput")
    wqkvT_h = nc.dram_tensor("wqkvT", [D, E3], F32, kind="ExternalInput")
    wswT_h = nc.dram_tensor("wswT", [D, ESW], F32, kind="ExternalInput")
    woutT_h = nc.dram_tensor("woutT", [D, D], F32, kind="ExternalInput")
    wg_h = nc.dram_tensor("wg", [DFF, D], F32, kind="ExternalInput")
    wu_h = nc.dram_tensor("wu", [DFF, D], F32, kind="ExternalInput")
    wd_h = nc.dram_tensor("wd", [D, DFF], F32, kind="ExternalInput")
    tri_h = nc.dram_tensor("tri", [128, 128], F32, kind="ExternalInput")
    idm_h = nc.dram_tensor("idm", [128, 128], F32, kind="ExternalInput")
    ones1_h = nc.dram_tensor("ones1", [1, 128], F32, kind="ExternalInput")
    ones128_h = nc.dram_tensor("ones128", [128, 1], F32, kind="ExternalInput")
    out_h = nc.dram_tensor("out", [T, D], F32, kind="ExternalOutput")
    qkvT_s = nc.dram_tensor("qkvT_s", [D, E3], BF16, kind="Internal")
    swT_s = nc.dram_tensor("swT_s", [D, ESW], BF16, kind="Internal")
    outT_s = nc.dram_tensor("outT_s", [D, D], BF16, kind="Internal")
    wgT_s = nc.dram_tensor("wgT_s", [D, DFF], BF16, kind="Internal")
    wuT_s = nc.dram_tensor("wuT_s", [D, DFF], BF16, kind="Internal")
    wdT_s = nc.dram_tensor("wdT_s", [DFF, D], BF16, kind="Internal")

    es = ExitStack()
    sb = lambda n, s_, d: es.enter_context(nc.sbuf_tensor(n, s_, d))
    psb = lambda n, s_, d: es.enter_context(nc.psum_tensor(n, s_, d))

    x_tm = sb("x_tm", [128, max(NJ * 1024, 4096)], F32)
    acc = sb("acc", [128, max(NJ * 1024, 4096)], F32)
    vp32 = sb("vp32", [128, 2048], F32)
    u1q = sb("u1q", [128, 2816], BF16)
    zc = sb("zc", [128, max(NJ * 1024, 2816)], F32)
    arena24 = sb("arena24", [128, max(NJ * 3072, 12288)], BF16)
    arenaQ = sb("arenaQ", [128, max(NJ * 2816, 8192)], BF16)
    aliasA = sb("aliasA", [128, max(NJ * 1024, 4096)], BF16)
    aliasB = sb("aliasB", [128, max(NJ * 1024, 4096)], BF16)
    aliasD = sb("aliasD", [128, max(KD * TT, 4096)], BF16)
    scr1 = sb("scr1", [128, 2048], F16)
    vpx = sb("vpx", [128, 2048], F16)
    scr2 = sb("scr2", [128, 2048], BF16)
    scr3 = sb("scr3", [128, 2048], BF16)
    scr4 = sb("scr4", [128, 2048], BF16)
    wbufA = sb("wbufA", [128, 12288], BF16)
    wbufB = sb("wbufB", [128, 12288], BF16)
    tri_b = sb("tri_b", [128, 128], BF16)
    idm_b = sb("idm_b", [128, 128], BF16)
    idm_f = sb("idm_f", [128, 128], F32)
    ones1_b = sb("ones1_b", [1, 128], BF16)
    ones128_b = sb("ones128_b", [128, 1], BF16)
    carryT = sb("carryT", [1, 2048], BF16)
    carry = [carryT.ap()[:, 0:2048], carryT.ap()[:, 0:2048]]
    s1r = sb("s1r", [128, NJ], F32)
    s2r = sb("s2r", [128, NJ], F32)
    r_r = sb("r_r", [128, NJ], F32)
    rr_r = sb("rr_r", [128, NJ], F32)
    alpha = sb("alpha", [128, NJ], F32)
    gamma = sb("gamma", [128, NJ], F32)
    beta = sb("beta", [128, NJ], F32)
    tmpr = sb("tmpr", [128, NJ], F32)
    tmpr2 = sb("tmpr2", [128, NJ], F32)
    qs2 = sb("qs2", [128, NJ], F32)
    ks2 = sb("ks2", [128, NJ], F32)
    rq_r = sb("rq_r", [128, NJ], F32)
    rk_r = sb("rk_r", [128, NJ], F32)
    bmrow = sb("bmrow", [128, 176], F32)
    scrow = bmrow
    shrow = sb("shrow", [128, 176], F32)
    rsrow = sb("rsrow", [128, 176], F32)
    negm = sb("negm", [128, NJ], F32)
    c_lneps = sb("c_lneps", [128, 1], F32)
    c_rmseps = sb("c_rmseps", [128, 1], F32)
    c_weps = sb("c_weps", [128, 1], F32)
    c_i1024 = sb("c_i1024", [128, 1], F32)
    c_neg1 = sb("c_neg1", [128, 1], F32)
    c_nhalf = sb("c_nhalf", [128, 1], F32)
    c_m1em5 = sb("c_m1em5", [128, 1], F32)
    c_1p = sb("c_1p", [128, 1], F32)
    c_m25 = sb("c_m25", [128, 1], F32)
    c_m35 = sb("c_m35", [128, 1], F32)
    c_m50 = sb("c_m50", [128, 1], F32)

    pall = psb("pall", [128, 3584], F32)
    pt = psb("pt", [128, 1024], BF16)

    p = Plan(nc)
    c_wA = Ctr(nc, es, "c_wA", 16)
    c_wB = Ctr(nc, es, "c_wB", 16)
    c_x = Ctr(nc, es, "c_x", 16)
    c_out = Ctr(nc, es, "c_out", 16)
    c_cst = Ctr(nc, es, "c_cst", 16)
    c_pA = Ctr(nc, es, "c_pA", 16)
    c_pB = Ctr(nc, es, "c_pB", 16)
    c_cA = Ctr(nc, es, "c_cA", 16)
    c_cB = Ctr(nc, es, "c_cB", 16)
    c_fo = Ctr(nc, es, "c_fo", 16)
    c_pe = Ctr(nc, es, "c_pe")
    c_dv = Ctr(nc, es, "c_dv")
    c_ac = Ctr(nc, es, "c_ac")
    c_pl = Ctr(nc, es, "c_pl")
    c_cr = Ctr(nc, es, "c_cr", 16)

    qkv_sb = arena24.ap()[:, 0:NJ * 3072].rearrange("p (j w) -> p j w", w=3072)
    h_v = arena24.ap()[:, 0:NJ * 2816].rearrange("p (j w) -> p j w", w=2816)
    hqT_v = arena24.ap()[:, 0:KF * TT].rearrange("p (k w) -> p k w", w=TT)
    hqT2 = arena24.ap()[:, 0:NJ * KF * 128].rearrange("p (j k w) -> p j k w", k=KF, w=128)
    wkv = arenaQ.ap()[:, 0:NJ * 2048].rearrange("p (j w) -> p j w", w=2048)
    hq_v = arenaQ.ap()[:, 0:NJ * 2816].rearrange("p (j w) -> p j w", w=2816)
    sigq = aliasA.ap()[:, 0:NJ * 1024].rearrange("p (j w) -> p j w", w=1024)
    xq_v = sigq
    y_v = aliasB.ap()[:, 0:NJ * 1024].rearrange("p (j w) -> p j w", w=1024)
    s_v = y_v
    fmD = aliasD.ap()[:, 0:KD * TT].rearrange("p (k w) -> p k w", w=TT)
    zc_v = zc.ap()[:, 0:NJ * 1024].rearrange("p (j w) -> p j w", w=1024)
    x_v = x_tm.ap()[:, 0:NJ * 1024].rearrange("p (j w) -> p j w", w=1024)
    acc_v = acc.ap()[:, 0:NJ * 1024].rearrange("p (j w) -> p j w", w=1024)

    def dma(eng, ctr, out_ap, in_ap, waits=()):
        p.step(eng, lambda e, o=out_ap, i=in_ap: e.dma_start(out=o, in_=i),
               waits=waits, incs=[(ctr, 1)])

    def mm(out_ap, lhsT_ap, rhs_ap, start, stop, waits=(), incs=()):
        p.step(Tn, lambda e, o=out_ap, l=lhsT_ap, r=rhs_ap, st=start, sp=stop:
               e.matmul(o, l, r, start=st, stop=sp), waits=waits, incs=incs)

    def act(func, out_ap, in_ap, bias=0.0, scale=1.0, accum=None, waits=(), incs=()):
        p.step(Y, lambda e, o=out_ap, i=in_ap, b=bias, s_=scale, a=accum, f=func:
               e.activation(o, i, f, bias=b, scale=s_, accum_out=a), waits=waits, incs=incs)

    def vts(out_ap, in_ap, s1_, s2_, op0, op1=None, waits=(), incs=()):
        if op1 is None:
            p.step(V, lambda e, o=out_ap, i=in_ap, a=s1_, o0=op0:
                   e.tensor_scalar(o, i, a, None, o0), waits=waits, incs=incs)
        else:
            p.step(V, lambda e, o=out_ap, i=in_ap, a=s1_, b=s2_, o0=op0, o1=op1:
                   e.tensor_scalar(o, i, a, b, o0, o1), waits=waits, incs=incs)

    def vtt(out_ap, a_ap, b_ap, op, waits=(), incs=()):
        p.step(V, lambda e, o=out_ap, a=a_ap, b=b_ap, oo=op:
               e.tensor_tensor(o, a, b, oo), waits=waits, incs=incs)

    def vsts(out_ap, in0_ap, scal, in1_ap, op0, op1, waits=(), incs=()):
        p.step(V, lambda e, o=out_ap, i0=in0_ap, q=scal, i1=in1_ap, o0=op0, o1=op1:
               e.scalar_tensor_tensor(o, i0, q, i1, o0, o1), waits=waits, incs=incs)

    def vcopy(out_ap, in_ap, waits=(), incs=()):
        p.step(V, lambda e, o=out_ap, i=in_ap: e.tensor_copy(o, i), waits=waits, incs=incs)

    def vrecip(out_ap, in_ap, waits=(), incs=()):
        p.step(V, lambda e, o=out_ap, i=in_ap: e.reciprocal(o, i), waits=waits, incs=incs)

    def vreduce(out_ap, in_ap, op, absval=False, waits=(), incs=()):
        p.step(V, lambda e, o=out_ap, i=in_ap, oo=op, av=absval: e.tensor_reduce(
            o, i, axis=AX.X, op=oo, apply_absolute_value=av) if av else e.tensor_reduce(
            o, i, axis=AX.X, op=oo), waits=waits, incs=incs)

    def ptt(out_ap, a_ap, b_ap, op, waits=(), incs=()):
        p.step(G, lambda e, o=out_ap, a=a_ap, b=b_ap, oo=op:
               e.tensor_tensor(o, a, b, oo), waits=waits, incs=incs)

    def pts(out_ap, in_ap, s1_, s2_, op0, op1=None, waits=(), incs=()):
        if op1 is None:
            p.step(G, lambda e, o=out_ap, i=in_ap, a=s1_, o0=op0:
                   e.tensor_scalar(o, i, a, None, o0), waits=waits, incs=incs)
        else:
            p.step(G, lambda e, o=out_ap, i=in_ap, a=s1_, b=s2_, o0=op0, o1=op1:
                   e.tensor_scalar(o, i, a, b, o0, o1), waits=waits, incs=incs)

    def psts(out_ap, in0_ap, scal, in1_ap, op0, op1, waits=(), incs=()):
        p.step(G, lambda e, o=out_ap, i0=in0_ap, q=scal, i1=in1_ap, o0=op0, o1=op1:
               e.scalar_tensor_tensor(o, i0, q, i1, o0, o1), waits=waits, incs=incs)

    def preduce(out_ap, in_ap, op, absval=False, waits=(), incs=()):
        p.step(G, lambda e, o=out_ap, i=in_ap, oo=op, av=absval: e.tensor_reduce(
            o, i, axis=AX.X, op=oo, apply_absolute_value=av) if av else e.tensor_reduce(
            o, i, axis=AX.X, op=oo), waits=waits, incs=incs)

    pt_guard = [None]

    def transpose_group(srcs, dst_ap, deps):
        for i, sap in enumerate(srcs):
            wt = []
            if i == 0:
                wt = list(deps)
                if pt_guard[0] is not None:
                    wt.append(pt_guard[0])
            p.step(Tn, lambda e, o=pt.ap()[:, i * 128:(i + 1) * 128], s_=sap:
                   e.transpose(o, s_, idm_b.ap()),
                   waits=wt, incs=[(c_pe, 1)] if i == len(srcs) - 1 else ())
        act(AF.Copy, dst_ap,
            pt.ap()[:, 0:len(srcs) * 128].rearrange("p (k w) -> p k w", w=128),
            waits=[(c_pe, None), (c_ac, None)], incs=[(c_ac, 1)])
        pt_guard[0] = (c_ac, c_ac.n)

    bank7 = [None] * 7
    rot = [0]
    rot_mod = [6]

    def bank_ap(b, width=512):
        return pall.ap()[:, b * 512:b * 512 + width]

    def next_bank(width=512):
        m = rot_mod[0]
        base = 4 if m == 3 else 0
        b = base + (rot[0] % m)
        rot[0] += 1
        guard = bank7[b]
        return bank_ap(b, width), b, ([] if guard is None else [guard])

    def quant_multi(jobs, waits=(), wave=2):
        for w in range(0, len(jobs), wave):
            _quant_wave(jobs[w:w + wave], waits)

    _qm_guard = {}

    def kw():
        wt = [(c_dv, None)]
        if pt_guard[0] is not None:
            wt.append(pt_guard[0])
        p.step(Tn, lambda e: e.transpose(pt.ap()[:, 0:128], idm_b.ap(), idm_b.ap()),
               waits=wt)

    def _quant_wave(jobs, waits=()):
        w0 = list(waits)
        set_guard = _qm_guard
        gs = [{} for _ in jobs]
        for i, (src_ap, dst_ap, fd, scf) in enumerate(jobs):
            gs[i]["nb"] = fd // 16
            gs[i]["sid"] = scf.get("set_id", i % 2)
        for i, (src_ap, dst_ap, fd, scf) in enumerate(jobs):
            nb = gs[i]["nb"]
            b16 = lambda ap2: ap2.rearrange("p (b k) -> p b k", k=16)
            gd = set_guard.get(gs[i]["sid"], [])
            vreduce(scf["bm"][:, 0:nb], b16(src_ap), A.max, absval=True,
                    waits=w0 + gd + [(c_dv, None), (c_pl, None)], incs=[(c_dv, 1)])
            if scf.get("ip"):
                vr = (c_dv, c_dv.n)
                p.step(Y, lambda e, o=scf["sg"], i2=src_ap: e.sign(o, i2),
                       waits=w0 + gd + [(c_ac, None)], incs=[(c_ac, 1)])
                act(AF.Abs, scf["vp"], src_ap,
                    waits=[vr, (c_ac, None)], incs=[(c_ac, 1)])
            else:
                act(AF.Abs, scf["vp"], src_ap,
                    waits=w0 + gd + [(c_ac, None)],
                    incs=[(c_ac, 1)])
                p.step(Y, lambda e, o=scf["sg"], i2=src_ap: e.sign(o, i2),
                       waits=[(c_ac, None)], incs=[(c_ac, 1)])
            gs[i]["abs"] = (c_ac, c_ac.n)
            kw()
        for i, (src_ap, dst_ap, fd, scf) in enumerate(jobs):
            vts(scf["sc"], scf["bm"], 1.0 / 6.0, 1e-12, A.mult, A.max,
                waits=[(c_dv, None), (c_pl, None)], incs=[(c_dv, 1)])
            vts(scf["sh"], scf["sc"], 0.5, None, A.mult, waits=[(c_dv, None)], incs=[(c_dv, 1)])
            vrecip(scf["rs"], scf["sc"], waits=[(c_dv, None)], incs=[(c_dv, 1)])
            gs[i]["rows"] = (c_dv, c_dv.n)
        for i, (src_ap, dst_ap, fd, scf) in enumerate(jobs):
            nb = gs[i]["nb"]
            b16 = lambda ap2: ap2.rearrange("p (b k) -> p b k", k=16)
            bcr = scf["rs"].unsqueeze(2).broadcast_to([128, nb, 16])
            ptt(b16(scf["vp"]), b16(scf["vp"]), bcr, A.mult,
                waits=[gs[i]["rows"], gs[i]["abs"], (c_pl, None)], incs=[(c_pl, 1)])
            gs[i]["vp"] = (c_pl, c_pl.n)
            kw()
        for i, (src_ap, dst_ap, fd, scf) in enumerate(jobs):
            act(AF.Sign, scf["q2"], scf["vp"], bias=c_m25.ap(),
                waits=[gs[i]["vp"], (c_ac, None)], incs=[(c_ac, 1)])
            act(AF.Sign, scf["z1"], scf["vp"], bias=c_m35.ap(),
                incs=[(c_ac, 1)])
            gs[i]["s56"] = (c_ac, c_ac.n)
        for i, (src_ap, dst_ap, fd, scf) in enumerate(jobs):
            vtt(scf["z1"], scf["q2"], scf["z1"], A.add,
                waits=[gs[i]["s56"], (c_dv, None)], incs=[(c_dv, 1)])
            gs[i]["t1"] = (c_dv, c_dv.n)
        for i, (src_ap, dst_ap, fd, scf) in enumerate(jobs):
            act(AF.Sign, scf["q2"], scf["vp"], bias=c_m50.ap(),
                waits=[gs[i]["t1"], (c_ac, None)], incs=[(c_ac, 1)])
            gs[i]["s7"] = (c_ac, c_ac.n)
        for i, (src_ap, dst_ap, fd, scf) in enumerate(jobs):
            vsts(scf["z1"], scf["q2"], 2.0, scf["z1"], A.mult, A.add,
                 waits=[gs[i]["s7"], (c_dv, None)], incs=[(c_dv, 1)])
            vts(scf["u1"], scf["vp"], 0.25, 4.0, A.is_gt, A.add,
                waits=[gs[i]["vp"], (c_dv, None), (c_pe, None), (c_ac, None)],
                incs=[(c_dv, 1)])
            for th in (0.75, 1.25, 1.75):
                vsts(scf["u1"], scf["vp"], th, scf["u1"], A.is_gt, A.add,
                     waits=[(c_dv, None)], incs=[(c_dv, 1)])
            kw()
            vtt(scf["q2"], scf["u1"], scf["z1"], A.add,
                waits=[(c_dv, None)], incs=[(c_dv, 1)])
            vtt(scf["q2"], scf["q2"], scf["sg"], A.mult,
                waits=[(c_dv, None)], incs=[(c_dv, 1)])
            gs[i]["q2g"] = (c_dv, c_dv.n)
            kw()
        for i, (src_ap, dst_ap, fd, scf) in enumerate(jobs):
            nb = gs[i]["nb"]
            b16 = lambda ap2: ap2.rearrange("p (b k) -> p b k", k=16)
            bcs = scf["sh"].unsqueeze(2).broadcast_to([128, nb, 16])
            ptt(b16(dst_ap), b16(scf["q2"]), bcs, A.mult,
                waits=[gs[i]["q2g"], (c_pl, None)], incs=[(c_pl, 1)])
            set_guard[gs[i]["sid"]] = [(c_pl, c_pl.n), (c_dv, gs[i]["q2g"][1]),
                                       (c_ac, gs[i]["s7"][1])]

    def quant(src_ap, dst_ap, fd, scf, waits=()):
        quant_multi([(src_ap, dst_ap, fd, scf)], waits=waits)

    dma(Y, c_cst, x_tm.ap()[:, 0:128], tri_h.ap())
    dma(Y, c_cst, x_tm.ap()[:, 128:256], idm_h.ap())
    dma(Y, c_cst, x_tm.ap()[0:1, 256:384], ones1_h.ap())
    dma(Y, c_cst, x_tm.ap()[:, 384:385], ones128_h.ap())
    vcopy(tri_b.ap(), x_tm.ap()[:, 0:128], waits=[(c_cst, None)], incs=[(c_dv, 1)])
    vcopy(idm_b.ap(), x_tm.ap()[:, 128:256], incs=[(c_dv, 1)])
    vcopy(idm_f.ap(), x_tm.ap()[:, 128:256], incs=[(c_dv, 1)])
    vcopy(ones1_b.ap(), x_tm.ap()[0:1, 256:384], incs=[(c_dv, 1)])
    vcopy(ones128_b.ap(), x_tm.ap()[:, 384:385], incs=[(c_dv, 1)])
    p.step(V, lambda e: e.memset(carry[0], 0.0), incs=[(c_dv, 1)])
    p.step(V, lambda e: e.memset(c_lneps.ap(), LN_EPS), incs=[(c_dv, 1)])
    p.step(V, lambda e: e.memset(c_rmseps.ap(), RMS_EPS), incs=[(c_dv, 1)])
    p.step(V, lambda e: e.memset(c_weps.ap(), W_EPS), incs=[(c_dv, 1)])
    p.step(V, lambda e: e.memset(c_i1024.ap(), 1.0 / 1024.0), incs=[(c_dv, 1)])
    p.step(V, lambda e: e.memset(c_neg1.ap(), -1.0), incs=[(c_dv, 1)])
    p.step(V, lambda e: e.memset(c_nhalf.ap(), -0.5), incs=[(c_dv, 1)])
    p.step(V, lambda e: e.memset(c_m1em5.ap(), -LN_EPS), incs=[(c_dv, 1)])
    p.step(V, lambda e: e.memset(c_1p.ap(), 1.0 + RMS_EPS), incs=[(c_dv, 1)])
    p.step(V, lambda e: e.memset(c_m25.ap(), -2.5), incs=[(c_dv, 1)])
    p.step(V, lambda e: e.memset(c_m35.ap(), -3.5), incs=[(c_dv, 1)])
    p.step(V, lambda e: e.memset(c_m50.ap(), -5.0), incs=[(c_dv, 1)])
    consts_dv = c_dv.n

    cast_tasks = []
    for (wh_, ws_, width_) in ((wqkvT_h, qkvT_s, E3), (wswT_h, swT_s, ESW),
                               (woutT_h, outT_s, D)):
        for ch_ in range(KD):
            cast_tasks.append((wh_, ws_, width_, ch_))
    zc_w = max(NJ * 1024, 2816)
    cregs = {"A": aliasA.ap()[:, 0:3072], "B": aliasB.ap()[:, 0:3072],
             "U": u1q.ap()[:, 0:2816]}
    cguard = {"A": [], "B": [], "U": []}
    cslotg = {"big": [], "small": []}
    crot = {"big": ["A", "B"], "small": ["U", "A", "B"], "i": 0, "j": 0}

    def emit_cast(task, big_slot, big_sem):
        wh_, ws_, width_, ch_ = task
        big = width_ > 2048
        slot, sem, sk = (big_slot, big_sem, "big") if big else             (vp32.ap()[:, 0:2048], c_wA, "small")
        dma(S, sem, slot[:, 0:width_], wh_.ap()[ch_ * 128:(ch_ + 1) * 128, :],
            waits=[(sem, sem.n)] + cslotg[sk])
        if big:
            rk = crot["big"][crot["i"] % 2]
            crot["i"] += 1
        else:
            rk = crot["small"][crot["j"] % 3]
            crot["j"] += 1
        cr = cregs[rk]
        vcopy(cr[:, 0:width_], slot[:, 0:width_],
              waits=[(sem, sem.n), (c_dv, None)] + cguard[rk], incs=[(c_dv, 1)])
        cslotg[sk] = [(c_dv, c_dv.n)]
        dma(S, c_fo, ws_.ap()[ch_ * 128:(ch_ + 1) * 128, :], cr[:, 0:width_],
            waits=[(c_fo, c_fo.n), (c_dv, None)])
        cguard[rk] = [(c_fo, c_fo.n)]

    interleave_casts = zc_w >= 3072
    if not interleave_casts:
        for task in cast_tasks:
            emit_cast(task, acc.ap()[:, 0:3072], c_cB)
        cast_tasks = []
    consts_g = [(c_dv, consts_dv)]

    gsl = [acc.ap()[:, 0:1024], acc.ap()[:, 1024:2048],
           x_tm.ap()[:, 0:1024], x_tm.ap()[:, 1024:2048]]
    gsem = [c_pA, c_pB, c_cA, c_cr]
    gguard = [list(cslotg["big"]), list(cslotg["big"]),
              consts_g + cslotg["big"], consts_g + cslotg["big"]]
    pro_sets = []
    ns4 = 4 if max(NJ * 1024, 2816) >= 4096 else 2
    for s4 in range(ns4):
        o = s4 * 1024
        pro_sets.append(dict(
            set_id=s4, ip=True,
            vp=gsl[s4],
            u1=aliasD.ap()[:, o:o + 1024],
            z1=arenaQ.ap()[:, o:o + 1024],
            q2=arenaQ.ap()[:, 4096 + o:4096 + o + 1024],
            sg=arena24.ap()[:, o:o + 1024],
            bm=acc.ap()[:, 2048 + s4 * 64:2048 + s4 * 64 + 64],
            sc=acc.ap()[:, 2048 + s4 * 64:2048 + s4 * 64 + 64],
            sh=acc.ap()[:, 2304 + s4 * 64:2304 + s4 * 64 + 64],
            rs=acc.ap()[:, 2560 + s4 * 64:2560 + s4 * 64 + 64],
            vp16=(scr1 if s4 < 2 else vpx).ap()[:, (s4 % 2) * 1024:(s4 % 2) * 1024 + 1024]))
    wqouts4 = [arena24.ap()[:, 4096 + s4 * 1024:4096 + s4 * 1024 + 1024] for s4 in range(4)]
    ttmp4 = [arena24.ap()[:, 8192 + s4 * 1024:8192 + s4 * 1024 + 1024] for s4 in range(4)]
    ldgs = {}
    pjob = 0

    NWV = 4 if ns4 == 4 else 2

    def _gu_wave(i0):
        jobs = []
        idxs4 = [i for i in range(i0, i0 + NWV)]
        for i in idxs4:
            sem_n, wsT, ch, sl = ldgs[i]
            jobs.append((gsl[sl], wqouts4[sl], 1024, pro_sets[sl % ns4]))
        quant_multi(jobs, waits=[ldgs[i][0] for i in idxs4] + [(c_fo, None), (c_cB, None)],
                    wave=NWV)
        qg = [(c_dv, c_dv.n), (c_pl, c_pl.n)]
        for i in idxs4:
            sem_n, wsT, ch, sl = ldgs[i]
            gguard[sl] = [(c_dv, c_dv.n), (c_ac, c_ac.n), (c_pl, c_pl.n)]
            transpose_group(
                [wqouts4[sl][:, fc * 128:(fc + 1) * 128] for fc in range(8)],
                ttmp4[sl].rearrange("p (k w) -> p k w", w=128),
                deps=qg + [(c_fo, None)])
            dst = wsT.ap()[:, ch * 128:(ch + 1) * 128].rearrange("(b p) w -> p b w", p=128)
            dma(S, c_fo, dst, ttmp4[sl].rearrange("p (b w) -> p b w", b=8),
                waits=[(c_fo, c_fo.n), (c_ac, None)])

    for (wh, wsT, nrows) in ((wg_h, wgT_s, DFF), (wu_h, wuT_s, DFF)):
        for ch in range(nrows // 128):
            sl = pjob % 4
            wts = [(gsem[sl], gsem[sl].n)] + list(gguard[sl])
            dma(S, gsem[sl], gsl[sl], wh.ap()[ch * 128:(ch + 1) * 128, :], waits=wts)
            ldgs[pjob] = ((gsem[sl], gsem[sl].n), wsT, ch, sl)
            pjob += 1
            if pjob % NWV == 0:
                _gu_wave(pjob - NWV)
                for _ in range(2):
                    if cast_tasks:
                        emit_cast(cast_tasks.pop(0), zc.ap()[:, 0:3072], c_cB)

    dslots = [acc.ap()[:, 0:2816], x_tm.ap()[:, 0:2816]]
    dsem = [c_pA, c_pB]
    dguard = [gguard[0] + gguard[1] + [(c_pl, None), (c_dv, None), (c_ac, None)],
              gguard[2] + gguard[3] + [(c_pl, None), (c_dv, None), (c_ac, None)]]
    down_sets = []
    for par in range(2):
        o = par * 1408
        down_sets.append(dict(
            set_id=par, ip=True,
            u1=u1q.ap()[:, o:o + 1408],
            z1=arenaQ.ap()[:, o:o + 1408],
            q2=arenaQ.ap()[:, 4096 + o:4096 + o + 1408],
            sg=arena24.ap()[:, o:o + 1408],
            bm=x_tm.ap()[:, 2864 + par * 88:2864 + par * 88 + 88],
            sc=x_tm.ap()[:, 2864 + par * 88:2864 + par * 88 + 88],
            sh=x_tm.ap()[:, 3040 + par * 88:3040 + par * 88 + 88],
            rs=x_tm.ap()[:, 3216 + par * 88:3216 + par * 88 + 88],
            vp16=(scr1 if par == 0 else vpx).ap()[:, 0:1408]))
    dwq = [arena24.ap()[:, 4096 + par * 1408:4096 + par * 1408 + 1408] for par in range(2)]
    ttmpd = arena24.ap()[:, 8192:8192 + 2816]
    for ch in range(D // 128):
        sl = ch % 2
        wts = [(dsem[sl], dsem[sl].n)] + list(dguard[sl])
        dma(S, dsem[sl], dslots[sl], wd_h.ap()[ch * 128:(ch + 1) * 128, :], waits=wts)
        ldg = (dsem[sl], dsem[sl].n)
        jobs = [(dslots[sl][:, hh * 1408:(hh + 1) * 1408], dwq[hh], 1408,
                 dict(down_sets[hh], vp=dslots[sl][:, hh * 1408:(hh + 1) * 1408]))
                for hh in range(2)]
        w0 = [ldg, (c_fo, None)]
        if ch == 0:
            w0 += [(c_dv, None), (c_ac, None), (c_pl, None), (c_pe, None)]
        quant_multi(jobs, waits=w0)
        dguard[sl] = [(c_dv, c_dv.n), (c_ac, c_ac.n), (c_pl, c_pl.n)]
        qg = [(c_dv, c_dv.n), (c_pl, c_pl.n)]
        nfc = DFF // 128
        for g in range((nfc + 7) // 8):
            f0, f1 = g * 8, min(g * 8 + 8, nfc)
            srcs = []
            for fc in range(f0, f1):
                hh = (fc * 128) // 1408
                off = fc * 128 - hh * 1408
                srcs.append(dwq[hh][:, off:off + 128])
            transpose_group(srcs, ttmpd[:, f0 * 128:f1 * 128].rearrange(
                "p (k w) -> p k w", w=128), deps=qg + [(c_fo, None)])
        dst = wdT_s.ap()[:, ch * 128:(ch + 1) * 128].rearrange("(b p) w -> p b w", p=128)
        dma(S, c_fo, dst, ttmpd[:, 0:nfc * 128].rearrange("p (b w) -> p b w", b=nfc),
            waits=[(c_fo, c_fo.n), (c_dv, None), (c_ac, None)])
        for _ in range(2):
            if cast_tasks:
                emit_cast(cast_tasks.pop(0), zc.ap()[:, 0:3072], c_cB)
    while cast_tasks:
        emit_cast(cast_tasks.pop(0), zc.ap()[:, 0:3072], c_cB)
    slot_guard = [None, (c_dv, c_dv.n)]
    slot_guard_pl = [None, (c_ac, c_ac.n)]
    pro_fo_g = (c_fo, c_fo.n)
    stages = [("qkv", qkvT_s, E3, KD), ("sw", swT_s, ESW, KD), ("out", outT_s, D, KD),
              ("gate", wgT_s, DFF, KD), ("up", wuT_s, DFF, KD), ("down", wdT_s, D, KF)]
    halves = []
    piece_map = {"qkv": (4, 4), "sw": (4, 4), "out": (4, 4),
                 "gate": (4, 4), "up": (4, 4), "down": (11, 11)}
    for t in range(NT):
        for (sname, ws, width, kc) in stages:
            k0 = 0
            for nk in piece_map[sname]:
                halves.append((ws, width, k0, nk))
                k0 += nk

    wstate = {"next": 0, "rel": [[pro_fo_g], [pro_fo_g]], "ld": {}}
    wsem = [c_wA, c_wB]

    def load_next_half():
        idx = wstate["next"]
        if idx >= len(halves):
            return
        ws, width, k0, nk = halves[idx]
        bi = idx % 2
        buf = wbufA if bi == 0 else wbufB
        wts = [(wsem[bi], wsem[bi].n)] + list(wstate["rel"][bi])
        if idx < 12 and ws in (wgT_s, wuT_s, wdT_s):
            wts.append(pro_fo_g)
        dma(S, wsem[bi], buf.ap()[:, 0:nk * width].rearrange("p (k w) -> p k w", w=width),
            ws.ap()[k0 * 128:(k0 + nk) * 128, :].rearrange("(k p) w -> p k w", p=128),
            waits=wts)
        wstate["ld"][idx] = (wsem[bi].n, buf, k0, nk, width)
        wstate["next"] = idx + 1

    def use_half(idx):
        ld, buf, k0, nk, width = wstate["ld"][idx]
        return (wsem[idx % 2], ld), buf.ap()[:, 0:nk * width].rearrange("p (k w) -> p k w", w=width), k0, nk

    def release_half(idx):
        wstate["rel"][idx % 2] = [(c_pe, c_pe.n)]
        load_next_half()

    load_next_half()
    load_next_half()
    hidx = [0]

    def stage_pieces(n):
        idxs = list(range(hidx[0], hidx[0] + n))
        hidx[0] += n
        infos = [use_half(i) for i in idxs]
        return idxs, infos

    def piece_for_k(infos, k):
        for gd, vw, k0, nk in infos:
            if k0 <= k < k0 + nk:
                return gd, vw, k0
        raise KeyError(k)

    def k_waits(infos, k, first_extra):
        wt = []
        if k == 0:
            wt += list(first_extra)
        for gd, vw, k0, nk in infos:
            if k == k0:
                wt.append(gd)
        return wt

    x_guard = [None]
    out_guard = [None]
    carry_ac_g = [None]
    sig_done = [None]
    kv_done = [None]

    def mm_stage(sname, lhs_fn, widths, evac_order, evac_fn, dep, KC):
        idxs, infos = stage_pieces(len(piece_map[sname]))
        ncf = len(widths)
        for j in range(NJ):
            banks = {}
            for ci_, c in enumerate(evac_order):
                n0, n1 = widths[c]
                pb, b, bw = next_bank(n1 - n0)
                banks[c] = (pb, b)
                for k in range(KC):
                    gd, wv, koff = piece_for_k(infos, k)
                    wt = list(bw) if k == 0 else []
                    if k == 0 and ci_ == 0:
                        wt.append(dep)
                    for gd2, _, k0_, _ in infos:
                        if k == k0_:
                            wt.append(gd2)
                    mm(pb, lhs_fn(k, j), wv[:, k - koff, n0:n1],
                       start=(k == 0), stop=(k == KC - 1), waits=wt,
                       incs=[(c_pe, 1)] if k == KC - 1 else ())
                pe_val = c_pe.n
                evac_fn(j, c, pb, b, pe_val)
        for i_ in idxs:
            release_half(i_)

    def emit_tile(t):
        jr = range(NJ)
        if t == 0:
            wx0 = [g for g in (slot_guard[1], slot_guard_pl[1]) if g is not None]
            dma(Y, c_x, x_v,
                x_h.ap()[0:TT, :].rearrange("(j p) d -> p j d", p=128),
                waits=wx0)
        xg = x_guard[0] if x_guard[0] is not None else (c_x, c_x.n)

        vreduce(s1r.ap(), x_v, A.add, waits=[xg, (c_dv, None)], incs=[(c_dv, 1)])
        vts(negm.ap(), s1r.ap(), -1.0 / 1024.0, None, A.mult,
            waits=[(c_dv, None)], incs=[(c_dv, 1)])
        negm_dv = c_dv.n
        junk = [scr1.ap()[:, 0:1024], scr1.ap()[:, 1024:2048],
                scr2.ap()[:, 0:1024], scr2.ap()[:, 1024:2048]]
        for j in jr:
            vts(zc_v[:, j, :], x_v[:, j, :], negm.ap()[:, j:j + 1], None, A.add,
                waits=[(c_dv, negm_dv)], incs=[(c_dv, 1)])
            act(AF.Square, junk[j % 4], x_v[:, j, :],
                bias=negm.ap()[:, j:j + 1], accum=s2r.ap()[:, j:j + 1],
                waits=[(c_dv, negm_dv), (c_ac, None)], incs=[(c_ac, 1)])
        zc_ac = c_ac.n
        zc_dv = c_dv.n
        act(AF.Ln, tmpr.ap(), s2r.ap(), bias=c_lneps.ap(), scale=c_i1024.ap(),
            waits=[(c_ac, None)], incs=[(c_ac, 1)])
        act(AF.Exp, r_r.ap(), tmpr.ap(), scale=c_nhalf.ap(),
            waits=[(c_ac, None)], incs=[(c_ac, 1)])
        act(AF.Square, tmpr2.ap(), r_r.ap(), waits=[(c_ac, None)], incs=[(c_ac, 1)])
        act(AF.Ln, tmpr.ap(), tmpr2.ap(), bias=c_1p.ap(), scale=c_m1em5.ap(),
            waits=[(c_ac, None)], incs=[(c_ac, 1)])
        act(AF.Exp, rr_r.ap(), tmpr.ap(), scale=c_nhalf.ap(),
            waits=[(c_ac, None)], incs=[(c_ac, 1)])
        rows_ac = c_ac.n
        vtt(alpha.ap(), r_r.ap(), rr_r.ap(), A.mult,
            waits=[(c_ac, rows_ac), (c_dv, None)], incs=[(c_dv, 1)])
        vts(gamma.ap(), r_r.ap(), 1.0, None, A.add, incs=[(c_dv, 1)])
        vtt(beta.ap(), negm.ap(), r_r.ap(), A.mult, incs=[(c_dv, 1)])
        rows_dv = c_dv.n
        for j in jr:
            wt = [(c_dv, rows_dv)]
            if out_guard[0] is not None:
                wt.append(out_guard[0])
            vts(acc_v[:, j, :], x_v[:, j, :],
                gamma.ap()[:, j:j + 1], beta.ap()[:, j:j + 1], A.mult, A.add,
                waits=wt, incs=[(c_dv, 1)])
        acc_dv = c_dv.n
        for j in jr:
            for g2 in range(2):
                pb, bnk, bw = next_bank(512)
                for fi in range(4):
                    fc = g2 * 4 + fi
                    wt = []
                    if fi == 0:
                        wt = [(c_dv, max(zc_dv, consts_dv))] + list(bw)
                    p.step(Tn, lambda e, o=pb[:, fi * 128:(fi + 1) * 128],
                           s_=zc_v[:, j, fc * 128:(fc + 1) * 128]:
                           e.transpose(o, s_, idm_f.ap()),
                           waits=wt, incs=[(c_pe, 1)] if fi == 3 else ())
                act(AF.Copy, fmD[:, g2 * 4:g2 * 4 + 4, j * 128:(j + 1) * 128],
                    pb.rearrange("p (k w) -> p k w", w=128)[:, 0:4, :],
                    waits=[(c_pe, None), (c_ac, None)], incs=[(c_ac, 1)])
                bank7[bnk] = (c_ac, c_ac.n)
        zcT_g = (c_ac, c_ac.n)

        qtmp = scr4.ap()[:, 1024:2048]
        ktmp = scr4.ap()[:, 0:1024]
        vtmp = scr3.ap()[:, 1024:2048]
        lnw = scr2.ap()[:, 0:1024]
        rinv = scr2.ap()[:, 1024:2048]
        idxs_q, infos_q = stage_pieces(len(piece_map["qkv"]))
        rot[0] = 0
        rot_mod[0] = 3
        yT_gs = []
        kv_vals = {}

        def qkv_grp(j):
            for c in range(6):
                pb, bq, bw = next_bank()
                for k in range(KD):
                    gd, wv, koff = piece_for_k(infos_q, k)
                    wt = list(bw) if k == 0 else []
                    if k == 0 and c == 0 and j == 0:
                        wt.append(zcT_g)
                    for gd2, _, k0_, _ in infos_q:
                        if k == k0_:
                            wt.append(gd2)
                    mm(pb, fmD[:, k, j * 128:(j + 1) * 128],
                       wv[:, k - koff, c * 512:(c + 1) * 512],
                       start=(k == 0), stop=(k == KD - 1),
                       waits=wt, incs=[(c_pe, 1)] if k == KD - 1 else ())
                pe_val = c_pe.n
                if c < 2:
                    dstp = qtmp[:, c * 512:(c + 1) * 512]
                elif c < 4:
                    dstp = ktmp[:, (c - 2) * 512:(c - 1) * 512]
                else:
                    dstp = vtmp[:, (c - 4) * 512:(c - 3) * 512]
                wt0 = [(c_pe, pe_val), (c_ac, None), (c_dv, rows_dv)]
                if c == 4:
                    wt0.append((c_cr, None))
                if c == 0 and sig_done[0] is not None:
                    wt0.append(sig_done[0])
                act(AF.Copy, dstp, pb, scale=alpha.ap()[:, j:j + 1],
                    waits=wt0, incs=[(c_ac, 1)])
                bank7[bq] = (c_ac, c_ac.n)
            act(AF.Square, scr3.ap()[:, 0:1024], qtmp, accum=qs2.ap()[:, j:j + 1],
                waits=[(c_ac, None)], incs=[(c_ac, 1)])
            act(AF.Square, scr3.ap()[:, 0:1024], ktmp, accum=ks2.ap()[:, j:j + 1],
                waits=[(c_ac, None)], incs=[(c_ac, 1)])
            act(AF.Ln, tmpr.ap()[:, j:j + 1], qs2.ap()[:, j:j + 1],
                bias=c_rmseps.ap(), scale=c_i1024.ap(),
                waits=[(c_ac, None)], incs=[(c_ac, 1)])
            act(AF.Ln, tmpr2.ap()[:, j:j + 1], ks2.ap()[:, j:j + 1],
                bias=c_rmseps.ap(), scale=c_i1024.ap(), incs=[(c_ac, 1)])
            act(AF.Exp, rq_r.ap()[:, j:j + 1], tmpr.ap()[:, j:j + 1],
                scale=c_nhalf.ap(), waits=[(c_ac, None)], incs=[(c_ac, 1)])
            act(AF.Exp, rk_r.ap()[:, j:j + 1], tmpr2.ap()[:, j:j + 1],
                scale=c_nhalf.ap(), incs=[(c_ac, 1)])
            act(AF.Sigmoid, sigq[:, j, :], qtmp, scale=rq_r.ap()[:, j:j + 1],
                waits=[(c_ac, None)], incs=[(c_ac, 1)])
            act(AF.Exp, wkv[:, j, 0:1024], ktmp, scale=rk_r.ap()[:, j:j + 1],
                incs=[(c_ac, 1)])
            vtt(wkv[:, j, 1024:2048], wkv[:, j, 0:1024], vtmp[:, 0:1024], A.mult,
                waits=[(c_ac, None), (c_dv, None)], incs=[(c_dv, 1)])
            sig_done[0] = (c_dv, c_dv.n)
            kv_vals[j] = c_dv.n

        def cumy_grp(j):
            gj = t * NJ + j
            cin = carry[gj % 2]
            cout = carry[(gj + 1) % 2]
            for q4 in range(4):
                pb = pall.ap()[:, q4 * 512:(q4 + 1) * 512]
                wt = []
                if bank7[q4] is not None:
                    wt.append(bank7[q4])
                    bank7[q4] = None
                if q4 == 0:
                    wt += [(c_dv, kv_vals[j])]
                    if carry_ac_g[0] is not None:
                        wt.append(carry_ac_g[0])
                mm(pb, tri_b.ap(), wkv[:, j, q4 * 512:(q4 + 1) * 512],
                   start=True, stop=False, waits=wt)
                mm(pb, ones1_b.ap(), cin[0:1, q4 * 512:(q4 + 1) * 512],
                   start=False, stop=True, incs=[(c_pe, 1)] if q4 == 3 else ())
            cums_pe = c_pe.n
            act(AF.Copy, scr3.ap()[0:64, 0:2048], pall.ap()[64:128, 0:2048],
                waits=[(c_pe, cums_pe), (c_ac, None), (c_dv, None)], incs=[(c_ac, 1)])
            dma(S, c_cr, cout, scr3.ap()[63:64, 0:2048],
                waits=[(c_cr, c_cr.n), (c_ac, None)])
            carry_ac_g[0] = (c_cr, c_cr.n)
            act(AF.Ln, lnw, pall.ap()[:, 0:1024], bias=c_weps.ap(),
                waits=[(c_ac, None)], incs=[(c_ac, 1)])
            vtt(y_v[:, j, :], sigq[:, j, :], pall.ap()[:, 1024:2048], A.mult,
                waits=[(c_pe, cums_pe), (c_dv, None)], incs=[(c_dv, 1)])
            act(AF.Exp, rinv, lnw, scale=c_neg1.ap(),
                waits=[(c_ac, None), (c_dv, None)], incs=[(c_ac, 1)])
            vtt(y_v[:, j, :], y_v[:, j, :], rinv, A.mult,
                waits=[(c_ac, None), (c_dv, None)], incs=[(c_dv, 1)])
            yev_dv = c_dv.n
            for q4 in range(4):
                bank7[q4] = (c_dv, yev_dv)
            transpose_group(
                [y_v[:, j, fc * 128:(fc + 1) * 128] for fc in range(KD)],
                fmD[:, :, j * 128:(j + 1) * 128],
                deps=[(c_dv, yev_dv)])
            yT_gs.append((c_ac, c_ac.n))

        sw_state = {}

        def sw_grp(j):
            if "pieces" not in sw_state:
                sw_state["pieces"] = stage_pieces(len(piece_map["sw"]))
            idxs_sw, infos_sw = sw_state["pieces"]
            sgt = scr1.ap()[:, (j % 2) * 1024:(j % 2) * 1024 + 1024]
            for ci_, c in enumerate((2, 3, 0, 1)):
                pb, bq, bw = next_bank()
                for k in range(KD):
                    gd, wv, koff = piece_for_k(infos_sw, k)
                    wt = list(bw) if k == 0 else []
                    if k == 0 and ci_ == 0:
                        wt.append(yT_gs[j])
                    for gd2, _, k0_, _ in infos_sw:
                        if k == k0_:
                            wt.append(gd2)
                    mm(pb, fmD[:, k, j * 128:(j + 1) * 128],
                       wv[:, k - koff, c * 512:(c + 1) * 512],
                       start=(k == 0), stop=(k == KD - 1), waits=wt,
                       incs=[(c_pe, 1)] if k == KD - 1 else ())
                if c >= 2:
                    act(AF.Silu, sgt[:, (c - 2) * 512:(c - 1) * 512], pb,
                        waits=[(c_pe, None), (c_ac, None), (c_dv, None)], incs=[(c_ac, 1)])
                    bank7[bq] = (c_ac, c_ac.n)
                else:
                    vtt(s_v[:, j, c * 512:(c + 1) * 512],
                        sgt[:, c * 512:(c + 1) * 512], pb, A.mult,
                        waits=[(c_pe, None), (c_ac, None), (c_dv, None)], incs=[(c_dv, 1)])
                    bank7[bq] = (c_dv, c_dv.n)
            transpose_group(
                [s_v[:, j, fc * 128:(fc + 1) * 128] for fc in range(KD)],
                fmD[:, :, j * 128:(j + 1) * 128],
                deps=[(c_dv, None)])

        if NJ == 4:
            qkv_grp(0)
            qkv_grp(1)
            cumy_grp(0)
            qkv_grp(2)
            cumy_grp(1)
            qkv_grp(3)
            cumy_grp(2)
            cumy_grp(3)
            for i_ in idxs_q:
                release_half(i_)
            sw_grp(0)
            sw_grp(1)
            sw_grp(2)
            sw_grp(3)
        else:
            for j in jr:
                qkv_grp(j)
            for j in jr:
                cumy_grp(j)
            for i_ in idxs_q:
                release_half(i_)
            for j in jr:
                sw_grp(j)
        rot[0] = 0
        rot_mod[0] = 6
        for i_ in sw_state["pieces"][0]:
            release_half(i_)
        sT_g = (c_ac, c_ac.n)

        xq_sets = []
        for par in range(2):
            o = par * 1024
            xq_sets.append(dict(
                set_id=par,
                vp=vp32.ap()[:, o:o + 1024],
                u1=scr1.ap()[:, o:o + 1024], z1=scr3.ap()[:, o:o + 1024],
                q2=scr2.ap()[:, o:o + 1024],
                sg=scr4.ap()[:, 1024:2048] if par == 0 else x_tm.ap()[:, 0:1024],
                bm=bmrow.ap()[:, par * 64:par * 64 + 64],
                sc=scrow.ap()[:, par * 64:par * 64 + 64],
                sh=shrow.ap()[:, par * 64:par * 64 + 64],
                rs=rsrow.ap()[:, par * 64:par * 64 + 64],
                vp16=vpx.ap()[:, o:o + 1024]))
        quant_multi([(zc_v[:, j2, :], xq_v[:, j2, :], 1024, xq_sets[j2 % 2])
                     for j2 in range(min(2, NJ))],
                    waits=[(c_ac, None), (c_dv, None), (c_cr, None),
                           (c_pl, None), (c_pe, None)])
        xq_g01 = [(c_dv, c_dv.n), (c_pl, c_pl.n)]

        def out_evac(j, c, pb, b, pe_val):
            vtt(acc_v[:, j, c * 512:(c + 1) * 512],
                acc_v[:, j, c * 512:(c + 1) * 512], pb, A.add,
                waits=[(c_pe, pe_val), (c_dv, acc_dv)], incs=[(c_dv, 1)])
            bank7[b] = (c_dv, c_dv.n)

        mm_stage("out", lambda k, j: fmD[:, k, j * 128:(j + 1) * 128],
                 [(0, 512), (512, 1024)], (0, 1), out_evac, sT_g, KD)
        o_dv = c_dv.n
        if NJ > 2:
            quant_multi([(zc_v[:, j2, :], xq_v[:, j2, :], 1024, xq_sets[j2 % 2])
                         for j2 in range(2, NJ)],
                        waits=[(c_ac, None)])
        xq_g23 = [(c_dv, c_dv.n), (c_pl, c_pl.n)]
        xqc_dv = c_dv.n
        xqc_pl = c_pl.n
        if t + 1 < NT:
            dma(Y, c_x, x_v,
                x_h.ap()[(t + 1) * TT:(t + 2) * TT, :].rearrange("(j p) d -> p j d", p=128),
                waits=[(c_x, c_x.n), (c_ac, None), (c_dv, xqc_dv), (c_pl, xqc_pl)])
            x_guard[0] = (c_x, c_x.n)

        NCF = (DFF + 511) // 512
        widthsF = [(c * 512, min((c + 1) * 512, DFF)) for c in range(NCF)]
        idxs_g, infos_g = stage_pieces(len(piece_map["gate"]))
        xqT_js = []
        for j in jr:
            transpose_group(
                [xq_v[:, j, fc * 128:(fc + 1) * 128] for fc in range(KD)],
                fmD[:, :, j * 128:(j + 1) * 128],
                deps=(xq_g01 if j < 2 else xq_g23) + [(c_dv, o_dv)])
            xqT_js.append((c_ac, c_ac.n))
            sgate = hq_v[:, j, :]
            for c in range(NCF):
                n0, n1 = widthsF[c]
                pb, bq, bw = next_bank(n1 - n0)
                for k in range(KD):
                    gd, wv, koff = piece_for_k(infos_g, k)
                    wt = list(bw) if k == 0 else []
                    if k == 0 and c == 0:
                        wt.append(xqT_js[j])
                    for gd2, _, k0_, _ in infos_g:
                        if k == k0_:
                            wt.append(gd2)
                    mm(pb, fmD[:, k, j * 128:(j + 1) * 128], wv[:, k - koff, n0:n1],
                       start=(k == 0), stop=(k == KD - 1), waits=wt,
                       incs=[(c_pe, 1)] if k == KD - 1 else ())
                act(AF.Silu, sgate[:, n0:n1], pb, scale=r_r.ap()[:, j:j + 1],
                    waits=[(c_pe, None), (c_ac, None), (c_dv, None)],
                    incs=[(c_ac, 1)])
                bank7[bq] = (c_ac, c_ac.n)
        for i_ in idxs_g:
            release_half(i_)
        xqT_g = (c_ac, c_ac.n)

        hq_sets = []
        for par in range(2):
            o = par * 1408
            hq_sets.append(dict(
                set_id=par,
                vp=zc.ap()[:, o:o + 1408],
                u1=u1q.ap()[:, o:o + 1408],
                z1=(scr2 if par == 0 else scr3).ap()[:, 0:1408],
                q2=aliasB.ap()[:, o:o + 1408], sg=aliasA.ap()[:, o:o + 1408],
                bm=bmrow.ap()[:, par * 88:par * 88 + 88],
                sc=scrow.ap()[:, par * 88:par * 88 + 88],
                sh=shrow.ap()[:, par * 88:par * 88 + 88],
                rs=rsrow.ap()[:, par * 88:par * 88 + 88],
                vp16=(scr1 if par == 0 else vpx).ap()[:, 0:1408]))
        idxs, infos = stage_pieces(len(piece_map["up"]))
        chain_gs = []
        for j in jr:
            sgate = hq_v[:, j, :]
            for c in range(NCF):
                n0, n1 = widthsF[c]
                pb, b, bw = next_bank(n1 - n0)
                for k in range(KD):
                    gd, wv, koff = piece_for_k(infos, k)
                    wt = list(bw) if k == 0 else []
                    if k == 0 and c == 0 and j == 0:
                        wt.append(xqT_g)
                    for gd2, _, k0_, _ in infos:
                        if k == k0_:
                            wt.append(gd2)
                    mm(pb, fmD[:, k, j * 128:(j + 1) * 128], wv[:, k - koff, n0:n1],
                       start=(k == 0), stop=(k == KD - 1), waits=wt,
                       incs=[(c_pe, 1)] if k == KD - 1 else ())
                vsts(h_v[:, j, n0:n1], pb, r_r.ap()[:, j:j + 1],
                     sgate[:, n0:n1], A.mult, A.mult,
                     waits=[(c_pe, None), (c_ac, None), (c_dv, None)],
                     incs=[(c_dv, 1)])
                bank7[b] = (c_dv, c_dv.n)
            hj_dv = c_dv.n
            quant_multi([(h_v[:, j, hh * 1408:(hh + 1) * 1408],
                          hq_v[:, j, hh * 1408:(hh + 1) * 1408],
                          1408, hq_sets[hh]) for hh in range(2)],
                        waits=[(c_dv, hj_dv), (c_ac, None)])
            chain_gs.append([(c_dv, c_dv.n), (c_pl, c_pl.n)])
        for i_ in idxs:
            release_half(i_)

        idxs, infos = stage_pieces(len(piece_map["down"]))
        for j in jr:
            for g in range(3):
                f0, f1 = g * 8, min(g * 8 + 8, KF)
                transpose_group(
                    [hq_v[:, j, fc * 128:(fc + 1) * 128] for fc in range(f0, f1)],
                    hqT2[:, j, f0:f1, :],
                    deps=chain_gs[j])
            hqT_j = (c_ac, c_ac.n)
            for c in range(2):
                pb, b, bw = next_bank()
                for k in range(KF):
                    gd, wv, koff = piece_for_k(infos, k)
                    wt = list(bw) if k == 0 else []
                    if k == 0:
                        wt.append(hqT_j)
                    for gd2, _, k0_, _ in infos:
                        if k == k0_:
                            wt.append(gd2)
                    mm(pb, hqT2[:, j, k, :], wv[:, k - koff, c * 512:(c + 1) * 512],
                       start=(k == 0), stop=(k == KF - 1), waits=wt,
                       incs=[(c_pe, 1)] if k == KF - 1 else ())
                wt2 = [(c_pe, None), (c_dv, o_dv)]
                if out_guard[0] is not None:
                    wt2.append(out_guard[0])
                vtt(acc_v[:, j, c * 512:(c + 1) * 512],
                    acc_v[:, j, c * 512:(c + 1) * 512], pb, A.add,
                    waits=wt2, incs=[(c_dv, 1)])
                bank7[b] = (c_dv, c_dv.n)
        for i_ in idxs:
            release_half(i_)
        dma(Y, c_out, out_h.ap()[t * TT:(t + 1) * TT, :].rearrange("(j p) d -> p j d", p=128),
            acc.ap()[:, 0:NJ * 1024].rearrange("p (j w) -> p j w", w=1024),
            waits=[(c_out, c_out.n), (c_dv, None)])
        out_guard[0] = (c_out, c_out.n)

    for t in range(NT):
        emit_tile(t)

    p.step(G, lambda e: e.nop(), waits=[(c_out, None)])

    with nc.Block() as block:
        @block.sync
        def _(e):
            for f in p.q["sync"]:
                f(e)

        @block.scalar
        def _(e):
            for f in p.q["scalar"]:
                f(e)

        @block.vector
        def _(e):
            for f in p.q["vector"]:
                f(e)

        @block.tensor
        def _(e):
            for f in p.q["tensor"]:
                f(e)

        @block.gpsimd
        def _(e):
            for f in p.q["gpsimd"]:
                f(e)

    es.close()
    return nc


_NC_CACHE = {}


def kernel(L0_attn_x, L0_attn_w_qkv, L0_attn_w_swiglu, L0_attn_w_out,
           L0_ffn_w_gate, L0_ffn_w_up, L0_ffn_w_down):
    x = np.ascontiguousarray(np.asarray(L0_attn_x, dtype=np.float32))
    B, T, _ = x.shape
    key = (T,)
    if key not in _NC_CACHE:
        _NC_CACHE[key] = build(TT=512, NT=T // 512)
    nc = _NC_CACHE[key]
    wqkvT = np.ascontiguousarray(np.asarray(L0_attn_w_qkv, dtype=np.float32).T)
    wswT = np.ascontiguousarray(np.asarray(L0_attn_w_swiglu, dtype=np.float32).T)
    woutT = np.ascontiguousarray(np.asarray(L0_attn_w_out, dtype=np.float32).T)
    wg = np.ascontiguousarray(np.asarray(L0_ffn_w_gate, dtype=np.float32))
    wu = np.ascontiguousarray(np.asarray(L0_ffn_w_up, dtype=np.float32))
    wd = np.ascontiguousarray(np.asarray(L0_ffn_w_down, dtype=np.float32))
    tri = np.triu(np.ones((128, 128), dtype=np.float32))
    idm = np.eye(128, dtype=np.float32)
    ones1 = np.ones((1, 128), dtype=np.float32)
    ones128 = np.ones((128, 1), dtype=np.float32)
    in_maps = []
    for b in range(B):
        in_maps.append({
            "x": np.ascontiguousarray(x[b]),
            "wqkvT": wqkvT, "wswT": wswT, "woutT": woutT,
            "wg": wg, "wu": wu, "wd": wd,
            "tri": tri, "idm": idm, "ones1": ones1, "ones128": ones128,
        })
    res = run_bass_kernel_spmd(nc, in_maps, core_ids=list(range(B)))
    out = np.stack([np.asarray(res.results[b]["out"]) for b in range(B)], axis=0)
    return out.astype(np.float32)


# version 29
# speedup vs baseline: 1.0571x; 1.0571x over previous
import numpy as np
from contextlib import ExitStack

import concourse.bass as bass
import concourse.mybir as mybir
from concourse.bass_utils import run_bass_kernel_spmd

F32 = mybir.dt.float32
BF16 = mybir.dt.bfloat16
F16 = mybir.dt.float16
A = mybir.AluOpType
AF = mybir.ActivationFunctionType
AX = mybir.AxisListType

D = 1024
E3 = 3072
ESW = 2048
DFF = 2816
KD = D // 128
KF = DFF // 128
LN_EPS = 1e-5
RMS_EPS = 1.1920929e-07
W_EPS = 1e-6

ENGS = ("sync", "scalar", "vector", "tensor", "gpsimd")
S, Y, V, Tn, G = ENGS


class Ctr:
    def __init__(self, nc, es, name, step=1):
        self.sem = es.enter_context(nc.semaphore(name))
        self.n = 0
        self.step = step

    def bump(self, k=1):
        self.n += k * self.step
        return self.n


class Plan:
    def __init__(self, nc):
        self.nc = nc
        self.q = {e: [] for e in ENGS}
        self.last_wait = {e: {} for e in ENGS}

    def step(self, eng, mk, waits=(), incs=()):
        wl = []
        for c, v in waits:
            val = c.n if v is None else v
            if val <= 0:
                continue
            if self.last_wait[eng].get(c.sem.name, -1) >= val:
                continue
            self.last_wait[eng][c.sem.name] = val
            wl.append((c.sem, val))
        il = []
        for c, k in incs:
            il.append((c.sem, k * c.step))
            c.bump(k)

        def f(e, wl=wl, il=il, mk=mk):
            for s_, v_ in wl:
                e.wait_ge(s_, v_)
            inst = mk(e)
            for s_, v_ in il:
                inst.then_inc(s_, v_)
        self.q[eng].append(f)


def build(TT=512, NT=8):
    T = TT * NT
    NJ = TT // 128
    nc = bass.Bass("TRN2", target_bir_lowering=False)

    x_h = nc.dram_tensor("x", [T, D], F32, kind="ExternalInput")
    wqkvT_h = nc.dram_tensor("wqkvT", [D, E3], F32, kind="ExternalInput")
    wswT_h = nc.dram_tensor("wswT", [D, ESW], F32, kind="ExternalInput")
    woutT_h = nc.dram_tensor("woutT", [D, D], F32, kind="ExternalInput")
    wg_h = nc.dram_tensor("wg", [DFF, D], F32, kind="ExternalInput")
    wu_h = nc.dram_tensor("wu", [DFF, D], F32, kind="ExternalInput")
    wd_h = nc.dram_tensor("wd", [D, DFF], F32, kind="ExternalInput")
    tri_h = nc.dram_tensor("tri", [128, 128], F32, kind="ExternalInput")
    idm_h = nc.dram_tensor("idm", [128, 128], F32, kind="ExternalInput")
    ones1_h = nc.dram_tensor("ones1", [1, 128], F32, kind="ExternalInput")
    ones128_h = nc.dram_tensor("ones128", [128, 1], F32, kind="ExternalInput")
    out_h = nc.dram_tensor("out", [T, D], F32, kind="ExternalOutput")
    qkvT_s = nc.dram_tensor("qkvT_s", [D, E3], BF16, kind="Internal")
    swT_s = nc.dram_tensor("swT_s", [D, ESW], BF16, kind="Internal")
    outT_s = nc.dram_tensor("outT_s", [D, D], BF16, kind="Internal")
    wgT_s = nc.dram_tensor("wgT_s", [D, DFF], BF16, kind="Internal")
    wuT_s = nc.dram_tensor("wuT_s", [D, DFF], BF16, kind="Internal")
    wdT_s = nc.dram_tensor("wdT_s", [DFF, D], BF16, kind="Internal")

    es = ExitStack()
    sb = lambda n, s_, d: es.enter_context(nc.sbuf_tensor(n, s_, d))
    psb = lambda n, s_, d: es.enter_context(nc.psum_tensor(n, s_, d))

    x_tm = sb("x_tm", [128, max(NJ * 1024, 4096)], F32)
    acc = sb("acc", [128, max(NJ * 1024, 4096)], F32)
    vp32 = sb("vp32", [128, 2048], F32)
    u1q = sb("u1q", [128, 2816], BF16)
    zc = sb("zc", [128, max(NJ * 1024, 2816)], F32)
    arena24 = sb("arena24", [128, max(NJ * 3072, 12288)], BF16)
    arenaQ = sb("arenaQ", [128, max(NJ * 2816, 8192)], BF16)
    aliasA = sb("aliasA", [128, max(NJ * 1024, 4096)], BF16)
    aliasB = sb("aliasB", [128, max(NJ * 1024, 4096)], BF16)
    aliasD = sb("aliasD", [128, max(KD * TT, 4096)], BF16)
    scr1 = sb("scr1", [128, 2048], F16)
    vpx = sb("vpx", [128, 2048], F16)
    scr2 = sb("scr2", [128, 2048], BF16)
    scr3 = sb("scr3", [128, 2048], BF16)
    scr4 = sb("scr4", [128, 2048], BF16)
    wbufA = sb("wbufA", [128, 12288], BF16)
    wbufB = sb("wbufB", [128, 12288], BF16)
    tri_b = sb("tri_b", [128, 128], BF16)
    idm_b = sb("idm_b", [128, 128], BF16)
    idm_f = sb("idm_f", [128, 128], F32)
    ones1_b = sb("ones1_b", [1, 128], BF16)
    ones128_b = sb("ones128_b", [128, 1], BF16)
    carryT = sb("carryT", [1, 2048], BF16)
    carry = [carryT.ap()[:, 0:2048], carryT.ap()[:, 0:2048]]
    s1r = sb("s1r", [128, NJ], F32)
    s2r = sb("s2r", [128, NJ], F32)
    r_r = sb("r_r", [128, NJ], F32)
    rr_r = sb("rr_r", [128, NJ], F32)
    alpha = sb("alpha", [128, NJ], F32)
    gamma = sb("gamma", [128, NJ], F32)
    beta = sb("beta", [128, NJ], F32)
    tmpr = sb("tmpr", [128, NJ], F32)
    tmpr2 = sb("tmpr2", [128, NJ], F32)
    qs2 = sb("qs2", [128, NJ], F32)
    ks2 = sb("ks2", [128, NJ], F32)
    rq_r = sb("rq_r", [128, NJ], F32)
    rk_r = sb("rk_r", [128, NJ], F32)
    bmrow = sb("bmrow", [128, 176], F32)
    scrow = bmrow
    shrow = sb("shrow", [128, 176], F32)
    rsrow = sb("rsrow", [128, 176], F32)
    negm = sb("negm", [128, NJ], F32)
    c_lneps = sb("c_lneps", [128, 1], F32)
    c_rmseps = sb("c_rmseps", [128, 1], F32)
    c_weps = sb("c_weps", [128, 1], F32)
    c_i1024 = sb("c_i1024", [128, 1], F32)
    c_neg1 = sb("c_neg1", [128, 1], F32)
    c_nhalf = sb("c_nhalf", [128, 1], F32)
    c_m1em5 = sb("c_m1em5", [128, 1], F32)
    c_1p = sb("c_1p", [128, 1], F32)
    c_m25 = sb("c_m25", [128, 1], F32)
    c_m35 = sb("c_m35", [128, 1], F32)
    c_m50 = sb("c_m50", [128, 1], F32)

    pall = psb("pall", [128, 3584], F32)
    pt = psb("pt", [128, 1024], BF16)

    p = Plan(nc)
    c_wA = Ctr(nc, es, "c_wA", 16)
    c_wB = Ctr(nc, es, "c_wB", 16)
    c_x = Ctr(nc, es, "c_x", 16)
    c_out = Ctr(nc, es, "c_out", 16)
    c_cst = Ctr(nc, es, "c_cst", 16)
    c_pA = Ctr(nc, es, "c_pA", 16)
    c_pB = Ctr(nc, es, "c_pB", 16)
    c_cA = Ctr(nc, es, "c_cA", 16)
    c_cB = Ctr(nc, es, "c_cB", 16)
    c_fo = Ctr(nc, es, "c_fo", 16)
    c_pe = Ctr(nc, es, "c_pe")
    c_dv = Ctr(nc, es, "c_dv")
    c_ac = Ctr(nc, es, "c_ac")
    c_pl = Ctr(nc, es, "c_pl")
    c_cr = Ctr(nc, es, "c_cr", 16)

    qkv_sb = arena24.ap()[:, 0:NJ * 3072].rearrange("p (j w) -> p j w", w=3072)
    h_v = arena24.ap()[:, 0:NJ * 2816].rearrange("p (j w) -> p j w", w=2816)
    hqT_v = arena24.ap()[:, 0:KF * TT].rearrange("p (k w) -> p k w", w=TT)
    hqT2 = arena24.ap()[:, 0:NJ * KF * 128].rearrange("p (j k w) -> p j k w", k=KF, w=128)
    wkv = arenaQ.ap()[:, 0:NJ * 2048].rearrange("p (j w) -> p j w", w=2048)
    hq_v = arenaQ.ap()[:, 0:NJ * 2816].rearrange("p (j w) -> p j w", w=2816)
    sigq = aliasA.ap()[:, 0:NJ * 1024].rearrange("p (j w) -> p j w", w=1024)
    xq_v = sigq
    y_v = aliasB.ap()[:, 0:NJ * 1024].rearrange("p (j w) -> p j w", w=1024)
    s_v = y_v
    fmD = aliasD.ap()[:, 0:KD * TT].rearrange("p (k w) -> p k w", w=TT)
    zc_v = zc.ap()[:, 0:NJ * 1024].rearrange("p (j w) -> p j w", w=1024)
    x_v = x_tm.ap()[:, 0:NJ * 1024].rearrange("p (j w) -> p j w", w=1024)
    acc_v = acc.ap()[:, 0:NJ * 1024].rearrange("p (j w) -> p j w", w=1024)

    def dma(eng, ctr, out_ap, in_ap, waits=()):
        p.step(eng, lambda e, o=out_ap, i=in_ap: e.dma_start(out=o, in_=i),
               waits=waits, incs=[(ctr, 1)])

    def mm(out_ap, lhsT_ap, rhs_ap, start, stop, waits=(), incs=()):
        p.step(Tn, lambda e, o=out_ap, l=lhsT_ap, r=rhs_ap, st=start, sp=stop:
               e.matmul(o, l, r, start=st, stop=sp), waits=waits, incs=incs)

    def act(func, out_ap, in_ap, bias=0.0, scale=1.0, accum=None, waits=(), incs=()):
        p.step(Y, lambda e, o=out_ap, i=in_ap, b=bias, s_=scale, a=accum, f=func:
               e.activation(o, i, f, bias=b, scale=s_, accum_out=a), waits=waits, incs=incs)

    def vts(out_ap, in_ap, s1_, s2_, op0, op1=None, waits=(), incs=()):
        if op1 is None:
            p.step(V, lambda e, o=out_ap, i=in_ap, a=s1_, o0=op0:
                   e.tensor_scalar(o, i, a, None, o0), waits=waits, incs=incs)
        else:
            p.step(V, lambda e, o=out_ap, i=in_ap, a=s1_, b=s2_, o0=op0, o1=op1:
                   e.tensor_scalar(o, i, a, b, o0, o1), waits=waits, incs=incs)

    def vtt(out_ap, a_ap, b_ap, op, waits=(), incs=()):
        p.step(V, lambda e, o=out_ap, a=a_ap, b=b_ap, oo=op:
               e.tensor_tensor(o, a, b, oo), waits=waits, incs=incs)

    def vsts(out_ap, in0_ap, scal, in1_ap, op0, op1, waits=(), incs=()):
        p.step(V, lambda e, o=out_ap, i0=in0_ap, q=scal, i1=in1_ap, o0=op0, o1=op1:
               e.scalar_tensor_tensor(o, i0, q, i1, o0, o1), waits=waits, incs=incs)

    def vcopy(out_ap, in_ap, waits=(), incs=()):
        p.step(V, lambda e, o=out_ap, i=in_ap: e.tensor_copy(o, i), waits=waits, incs=incs)

    def vrecip(out_ap, in_ap, waits=(), incs=()):
        p.step(V, lambda e, o=out_ap, i=in_ap: e.reciprocal(o, i), waits=waits, incs=incs)

    def vreduce(out_ap, in_ap, op, absval=False, waits=(), incs=()):
        p.step(V, lambda e, o=out_ap, i=in_ap, oo=op, av=absval: e.tensor_reduce(
            o, i, axis=AX.X, op=oo, apply_absolute_value=av) if av else e.tensor_reduce(
            o, i, axis=AX.X, op=oo), waits=waits, incs=incs)

    def ptt(out_ap, a_ap, b_ap, op, waits=(), incs=()):
        p.step(G, lambda e, o=out_ap, a=a_ap, b=b_ap, oo=op:
               e.tensor_tensor(o, a, b, oo), waits=waits, incs=incs)

    def pts(out_ap, in_ap, s1_, s2_, op0, op1=None, waits=(), incs=()):
        if op1 is None:
            p.step(G, lambda e, o=out_ap, i=in_ap, a=s1_, o0=op0:
                   e.tensor_scalar(o, i, a, None, o0), waits=waits, incs=incs)
        else:
            p.step(G, lambda e, o=out_ap, i=in_ap, a=s1_, b=s2_, o0=op0, o1=op1:
                   e.tensor_scalar(o, i, a, b, o0, o1), waits=waits, incs=incs)

    def psts(out_ap, in0_ap, scal, in1_ap, op0, op1, waits=(), incs=()):
        p.step(G, lambda e, o=out_ap, i0=in0_ap, q=scal, i1=in1_ap, o0=op0, o1=op1:
               e.scalar_tensor_tensor(o, i0, q, i1, o0, o1), waits=waits, incs=incs)

    def preduce(out_ap, in_ap, op, absval=False, waits=(), incs=()):
        p.step(G, lambda e, o=out_ap, i=in_ap, oo=op, av=absval: e.tensor_reduce(
            o, i, axis=AX.X, op=oo, apply_absolute_value=av) if av else e.tensor_reduce(
            o, i, axis=AX.X, op=oo), waits=waits, incs=incs)

    pt_guard = [None]

    def transpose_group(srcs, dst_ap, deps):
        for i, sap in enumerate(srcs):
            wt = []
            if i == 0:
                wt = list(deps)
                if pt_guard[0] is not None:
                    wt.append(pt_guard[0])
            p.step(Tn, lambda e, o=pt.ap()[:, i * 128:(i + 1) * 128], s_=sap:
                   e.transpose(o, s_, idm_b.ap()),
                   waits=wt, incs=[(c_pe, 1)] if i == len(srcs) - 1 else ())
        act(AF.Copy, dst_ap,
            pt.ap()[:, 0:len(srcs) * 128].rearrange("p (k w) -> p k w", w=128),
            waits=[(c_pe, None), (c_ac, None)], incs=[(c_ac, 1)])
        pt_guard[0] = (c_ac, c_ac.n)

    bank7 = [None] * 7
    rot = [0]
    rot_mod = [6]

    def bank_ap(b, width=512):
        return pall.ap()[:, b * 512:b * 512 + width]

    def next_bank(width=512):
        m = rot_mod[0]
        base = 4 if m == 3 else 0
        b = base + (rot[0] % m)
        rot[0] += 1
        guard = bank7[b]
        return bank_ap(b, width), b, ([] if guard is None else [guard])

    def quant_multi(jobs, waits=(), wave=2):
        for w in range(0, len(jobs), wave):
            _quant_wave(jobs[w:w + wave], waits)

    _qm_guard = {}

    def _quant_wave(jobs, waits=()):
        gs = _qw_s1(jobs, waits)
        _qw_s2(jobs, gs)

    def _qw_s1(jobs, waits=()):
        w0 = list(waits)
        set_guard = _qm_guard
        gs = [{} for _ in jobs]
        for i, (src_ap, dst_ap, fd, scf) in enumerate(jobs):
            gs[i]["nb"] = fd // 16
            gs[i]["sid"] = scf.get("set_id", i % 2)
        for i, (src_ap, dst_ap, fd, scf) in enumerate(jobs):
            nb = gs[i]["nb"]
            b16 = lambda ap2: ap2.rearrange("p (b k) -> p b k", k=16)
            gd = set_guard.get(gs[i]["sid"], [])
            vreduce(scf["bm"][:, 0:nb], b16(src_ap), A.max, absval=True,
                    waits=w0 + gd + [(c_dv, None), (c_pl, None)], incs=[(c_dv, 1)])
            if scf.get("ip"):
                vr = (c_dv, c_dv.n)
                p.step(Y, lambda e, o=scf["sg"], i2=src_ap: e.sign(o, i2),
                       waits=w0 + gd + [(c_ac, None)], incs=[(c_ac, 1)])
                act(AF.Abs, scf["vp"], src_ap,
                    waits=[vr, (c_ac, None)], incs=[(c_ac, 1)])
            else:
                act(AF.Abs, scf["vp"], src_ap,
                    waits=w0 + gd + [(c_ac, None)],
                    incs=[(c_ac, 1)])
                p.step(Y, lambda e, o=scf["sg"], i2=src_ap: e.sign(o, i2),
                       waits=[(c_ac, None)], incs=[(c_ac, 1)])
            gs[i]["abs"] = (c_ac, c_ac.n)
        for i, (src_ap, dst_ap, fd, scf) in enumerate(jobs):
            vts(scf["sc"], scf["bm"], 1.0 / 6.0, 1e-12, A.mult, A.max,
                waits=[(c_dv, None), (c_pl, None)], incs=[(c_dv, 1)])
            vts(scf["sh"], scf["sc"], 0.5, None, A.mult, waits=[(c_dv, None)], incs=[(c_dv, 1)])
            vrecip(scf["rs"], scf["sc"], waits=[(c_dv, None)], incs=[(c_dv, 1)])
            gs[i]["rows"] = (c_dv, c_dv.n)
        for i, (src_ap, dst_ap, fd, scf) in enumerate(jobs):
            nb = gs[i]["nb"]
            b16 = lambda ap2: ap2.rearrange("p (b k) -> p b k", k=16)
            bcr = scf["rs"].unsqueeze(2).broadcast_to([128, nb, 16])
            ptt(b16(scf["vp"]), b16(scf["vp"]), bcr, A.mult,
                waits=[gs[i]["rows"], gs[i]["abs"], (c_pl, None)], incs=[(c_pl, 1)])
            gs[i]["vp"] = (c_pl, c_pl.n)
        return gs

    def _qw_s2(jobs, gs):
        set_guard = _qm_guard
        for i, (src_ap, dst_ap, fd, scf) in enumerate(jobs):
            act(AF.Sign, scf["q2"], scf["vp"], bias=c_m25.ap(),
                waits=[gs[i]["vp"], (c_ac, None)], incs=[(c_ac, 1)])
            act(AF.Sign, scf["z1"], scf["vp"], bias=c_m35.ap(),
                incs=[(c_ac, 1)])
            gs[i]["s56"] = (c_ac, c_ac.n)
        for i, (src_ap, dst_ap, fd, scf) in enumerate(jobs):
            vtt(scf["z1"], scf["q2"], scf["z1"], A.add,
                waits=[gs[i]["s56"], (c_dv, None)], incs=[(c_dv, 1)])
            gs[i]["t1"] = (c_dv, c_dv.n)
        for i, (src_ap, dst_ap, fd, scf) in enumerate(jobs):
            act(AF.Sign, scf["q2"], scf["vp"], bias=c_m50.ap(),
                waits=[gs[i]["t1"], (c_ac, None)], incs=[(c_ac, 1)])
            gs[i]["s7"] = (c_ac, c_ac.n)
        for i, (src_ap, dst_ap, fd, scf) in enumerate(jobs):
            vsts(scf["z1"], scf["q2"], 2.0, scf["z1"], A.mult, A.add,
                 waits=[gs[i]["s7"], (c_dv, None)], incs=[(c_dv, 1)])
            vts(scf["u1"], scf["vp"], 0.25, 4.0, A.is_gt, A.add,
                waits=[gs[i]["vp"], (c_dv, None), (c_pe, None), (c_ac, None)],
                incs=[(c_dv, 1)])
            for th in (0.75, 1.25, 1.75):
                vsts(scf["u1"], scf["vp"], th, scf["u1"], A.is_gt, A.add,
                     waits=[(c_dv, None)], incs=[(c_dv, 1)])
            vtt(scf["q2"], scf["u1"], scf["z1"], A.add,
                waits=[(c_dv, None)], incs=[(c_dv, 1)])
            vtt(scf["q2"], scf["q2"], scf["sg"], A.mult,
                waits=[(c_dv, None)], incs=[(c_dv, 1)])
            gs[i]["q2g"] = (c_dv, c_dv.n)
        for i, (src_ap, dst_ap, fd, scf) in enumerate(jobs):
            nb = gs[i]["nb"]
            b16 = lambda ap2: ap2.rearrange("p (b k) -> p b k", k=16)
            bcs = scf["sh"].unsqueeze(2).broadcast_to([128, nb, 16])
            ptt(b16(dst_ap), b16(scf["q2"]), bcs, A.mult,
                waits=[gs[i]["q2g"], (c_pl, None)], incs=[(c_pl, 1)])
            set_guard[gs[i]["sid"]] = [(c_pl, c_pl.n), (c_dv, gs[i]["q2g"][1]),
                                       (c_ac, gs[i]["s7"][1])]

    def quant(src_ap, dst_ap, fd, scf, waits=()):
        quant_multi([(src_ap, dst_ap, fd, scf)], waits=waits)

    dma(Y, c_cst, x_tm.ap()[:, 0:128], tri_h.ap())
    dma(Y, c_cst, x_tm.ap()[:, 128:256], idm_h.ap())
    dma(Y, c_cst, x_tm.ap()[0:1, 256:384], ones1_h.ap())
    dma(Y, c_cst, x_tm.ap()[:, 384:385], ones128_h.ap())
    vcopy(tri_b.ap(), x_tm.ap()[:, 0:128], waits=[(c_cst, None)], incs=[(c_dv, 1)])
    vcopy(idm_b.ap(), x_tm.ap()[:, 128:256], incs=[(c_dv, 1)])
    vcopy(idm_f.ap(), x_tm.ap()[:, 128:256], incs=[(c_dv, 1)])
    vcopy(ones1_b.ap(), x_tm.ap()[0:1, 256:384], incs=[(c_dv, 1)])
    vcopy(ones128_b.ap(), x_tm.ap()[:, 384:385], incs=[(c_dv, 1)])
    p.step(V, lambda e: e.memset(carry[0], 0.0), incs=[(c_dv, 1)])
    p.step(V, lambda e: e.memset(c_lneps.ap(), LN_EPS), incs=[(c_dv, 1)])
    p.step(V, lambda e: e.memset(c_rmseps.ap(), RMS_EPS), incs=[(c_dv, 1)])
    p.step(V, lambda e: e.memset(c_weps.ap(), W_EPS), incs=[(c_dv, 1)])
    p.step(V, lambda e: e.memset(c_i1024.ap(), 1.0 / 1024.0), incs=[(c_dv, 1)])
    p.step(V, lambda e: e.memset(c_neg1.ap(), -1.0), incs=[(c_dv, 1)])
    p.step(V, lambda e: e.memset(c_nhalf.ap(), -0.5), incs=[(c_dv, 1)])
    p.step(V, lambda e: e.memset(c_m1em5.ap(), -LN_EPS), incs=[(c_dv, 1)])
    p.step(V, lambda e: e.memset(c_1p.ap(), 1.0 + RMS_EPS), incs=[(c_dv, 1)])
    p.step(V, lambda e: e.memset(c_m25.ap(), -2.5), incs=[(c_dv, 1)])
    p.step(V, lambda e: e.memset(c_m35.ap(), -3.5), incs=[(c_dv, 1)])
    p.step(V, lambda e: e.memset(c_m50.ap(), -5.0), incs=[(c_dv, 1)])
    consts_dv = c_dv.n

    cast_tasks = []
    for (wh_, ws_, width_) in ((wqkvT_h, qkvT_s, E3), (wswT_h, swT_s, ESW),
                               (woutT_h, outT_s, D)):
        for ch_ in range(KD):
            cast_tasks.append((wh_, ws_, width_, ch_))
    zc_w = max(NJ * 1024, 2816)
    cregs = {"A": aliasA.ap()[:, 0:3072], "B": aliasB.ap()[:, 0:3072],
             "U": u1q.ap()[:, 0:2816]}
    cguard = {"A": [], "B": [], "U": []}
    cslotg = {"big": [], "small": []}
    crot = {"big": ["A", "B"], "small": ["U", "A", "B"], "i": 0, "j": 0}

    def emit_cast(task, big_slot, big_sem):
        wh_, ws_, width_, ch_ = task
        big = width_ > 2048
        slot, sem, sk = (big_slot, big_sem, "big") if big else             (vp32.ap()[:, 0:2048], c_wA, "small")
        dma(S, sem, slot[:, 0:width_], wh_.ap()[ch_ * 128:(ch_ + 1) * 128, :],
            waits=[(sem, sem.n)] + cslotg[sk])
        if big:
            rk = crot["big"][crot["i"] % 2]
            crot["i"] += 1
        else:
            rk = crot["small"][crot["j"] % 3]
            crot["j"] += 1
        cr = cregs[rk]
        vcopy(cr[:, 0:width_], slot[:, 0:width_],
              waits=[(sem, sem.n), (c_dv, None)] + cguard[rk], incs=[(c_dv, 1)])
        cslotg[sk] = [(c_dv, c_dv.n)]
        dma(S, c_fo, ws_.ap()[ch_ * 128:(ch_ + 1) * 128, :], cr[:, 0:width_],
            waits=[(c_fo, c_fo.n), (c_dv, None)])
        cguard[rk] = [(c_fo, c_fo.n)]

    interleave_casts = zc_w >= 3072
    if not interleave_casts:
        for task in cast_tasks:
            emit_cast(task, acc.ap()[:, 0:3072], c_cB)
        cast_tasks = []
    consts_g = [(c_dv, consts_dv)]

    gsl = [acc.ap()[:, 0:1024], acc.ap()[:, 1024:2048],
           x_tm.ap()[:, 0:1024], x_tm.ap()[:, 1024:2048]]
    gsem = [c_pA, c_pB, c_cA, c_cr]
    gguard = [list(cslotg["big"]), list(cslotg["big"]),
              consts_g + cslotg["big"], consts_g + cslotg["big"]]
    pro_sets = []
    ns4 = 4 if max(NJ * 1024, 2816) >= 4096 else 2
    for s4 in range(ns4):
        o = s4 * 1024
        pro_sets.append(dict(
            set_id=s4, ip=True,
            vp=gsl[s4],
            u1=aliasD.ap()[:, o:o + 1024],
            z1=arenaQ.ap()[:, o:o + 1024],
            q2=arenaQ.ap()[:, 4096 + o:4096 + o + 1024],
            sg=arena24.ap()[:, o:o + 1024],
            bm=acc.ap()[:, 2048 + s4 * 64:2048 + s4 * 64 + 64],
            sc=acc.ap()[:, 2048 + s4 * 64:2048 + s4 * 64 + 64],
            sh=acc.ap()[:, 2304 + s4 * 64:2304 + s4 * 64 + 64],
            rs=acc.ap()[:, 2560 + s4 * 64:2560 + s4 * 64 + 64],
            vp16=(scr1 if s4 < 2 else vpx).ap()[:, (s4 % 2) * 1024:(s4 % 2) * 1024 + 1024]))
    wqouts4 = [arena24.ap()[:, 4096 + s4 * 1024:4096 + s4 * 1024 + 1024] for s4 in range(4)]
    ttmp4 = [arena24.ap()[:, 8192 + s4 * 1024:8192 + s4 * 1024 + 1024] for s4 in range(4)]
    ldgs = {}
    pjob = 0

    def _gu_s1(i0):
        jobs = []
        for i in (i0, i0 + 1):
            sem_n, wsT, ch, sl = ldgs[i]
            jobs.append((gsl[sl], wqouts4[sl], 1024, pro_sets[sl % ns4]))
        gsx = _qw_s1(jobs, waits=[ldgs[i0][0], ldgs[i0 + 1][0],
                                  (c_fo, None), (c_cB, None)])
        return jobs, gsx

    def _gu_s2(i0, jobs, gsx):
        _qw_s2(jobs, gsx)
        qg = [(c_dv, c_dv.n), (c_pl, c_pl.n)]
        for i in (i0, i0 + 1):
            sem_n, wsT, ch, sl = ldgs[i]
            gguard[sl] = [(c_dv, c_dv.n), (c_ac, c_ac.n), (c_pl, c_pl.n)]
            transpose_group(
                [wqouts4[sl][:, fc * 128:(fc + 1) * 128] for fc in range(8)],
                ttmp4[sl].rearrange("p (k w) -> p k w", w=128),
                deps=qg + [(c_fo, None)])
            dst = wsT.ap()[:, ch * 128:(ch + 1) * 128].rearrange("(b p) w -> p b w", p=128)
            dma(S, c_fo, dst, ttmp4[sl].rearrange("p (b w) -> p b w", b=8),
                waits=[(c_fo, c_fo.n), (c_ac, None)])
        for _ in range(2):
            if cast_tasks:
                emit_cast(cast_tasks.pop(0), zc.ap()[:, 0:3072], c_cB)

    pend = [None]
    for (wh, wsT, nrows) in ((wg_h, wgT_s, DFF), (wu_h, wuT_s, DFF)):
        for ch in range(nrows // 128):
            sl = pjob % 4
            wts = [(gsem[sl], gsem[sl].n)] + list(gguard[sl])
            dma(S, gsem[sl], gsl[sl], wh.ap()[ch * 128:(ch + 1) * 128, :], waits=wts)
            ldgs[pjob] = ((gsem[sl], gsem[sl].n), wsT, ch, sl)
            pjob += 1
            if pjob % 2 == 0:
                if ns4 == 4:
                    jobs, gsx = _gu_s1(pjob - 2)
                    if pend[0] is not None:
                        _gu_s2(*pend[0])
                    pend[0] = (pjob - 2, jobs, gsx)
                else:
                    jobs, gsx = _gu_s1(pjob - 2)
                    _gu_s2(pjob - 2, jobs, gsx)
    if pend[0] is not None:
        _gu_s2(*pend[0])
        pend[0] = None

    dslots = [acc.ap()[:, 0:2816], x_tm.ap()[:, 0:2816]]
    dsem = [c_pA, c_pB]
    dguard = [gguard[0] + gguard[1] + [(c_pl, None), (c_dv, None), (c_ac, None)],
              gguard[2] + gguard[3] + [(c_pl, None), (c_dv, None), (c_ac, None)]]
    down_sets = []
    for par in range(2):
        o = par * 1408
        down_sets.append(dict(
            set_id=par, ip=True,
            u1=u1q.ap()[:, o:o + 1408],
            z1=arenaQ.ap()[:, o:o + 1408],
            q2=arenaQ.ap()[:, 4096 + o:4096 + o + 1408],
            sg=arena24.ap()[:, o:o + 1408],
            bm=x_tm.ap()[:, 2864 + par * 88:2864 + par * 88 + 88],
            sc=x_tm.ap()[:, 2864 + par * 88:2864 + par * 88 + 88],
            sh=x_tm.ap()[:, 3040 + par * 88:3040 + par * 88 + 88],
            rs=x_tm.ap()[:, 3216 + par * 88:3216 + par * 88 + 88],
            vp16=(scr1 if par == 0 else vpx).ap()[:, 0:1408]))
    dwq = [arena24.ap()[:, 4096 + par * 1408:4096 + par * 1408 + 1408] for par in range(2)]
    ttmpd = arena24.ap()[:, 8192:8192 + 2816]
    for ch in range(D // 128):
        sl = ch % 2
        wts = [(dsem[sl], dsem[sl].n)] + list(dguard[sl])
        dma(S, dsem[sl], dslots[sl], wd_h.ap()[ch * 128:(ch + 1) * 128, :], waits=wts)
        ldg = (dsem[sl], dsem[sl].n)
        jobs = [(dslots[sl][:, hh * 1408:(hh + 1) * 1408], dwq[hh], 1408,
                 dict(down_sets[hh], vp=dslots[sl][:, hh * 1408:(hh + 1) * 1408]))
                for hh in range(2)]
        w0 = [ldg, (c_fo, None)]
        if ch == 0:
            w0 += [(c_dv, None), (c_ac, None), (c_pl, None), (c_pe, None)]
        quant_multi(jobs, waits=w0)
        dguard[sl] = [(c_dv, c_dv.n), (c_ac, c_ac.n), (c_pl, c_pl.n)]
        qg = [(c_dv, c_dv.n), (c_pl, c_pl.n)]
        nfc = DFF // 128
        for g in range((nfc + 7) // 8):
            f0, f1 = g * 8, min(g * 8 + 8, nfc)
            srcs = []
            for fc in range(f0, f1):
                hh = (fc * 128) // 1408
                off = fc * 128 - hh * 1408
                srcs.append(dwq[hh][:, off:off + 128])
            transpose_group(srcs, ttmpd[:, f0 * 128:f1 * 128].rearrange(
                "p (k w) -> p k w", w=128), deps=qg + [(c_fo, None)])
        dst = wdT_s.ap()[:, ch * 128:(ch + 1) * 128].rearrange("(b p) w -> p b w", p=128)
        dma(S, c_fo, dst, ttmpd[:, 0:nfc * 128].rearrange("p (b w) -> p b w", b=nfc),
            waits=[(c_fo, c_fo.n), (c_dv, None), (c_ac, None)])
        for _ in range(2):
            if cast_tasks:
                emit_cast(cast_tasks.pop(0), zc.ap()[:, 0:3072], c_cB)
    while cast_tasks:
        emit_cast(cast_tasks.pop(0), zc.ap()[:, 0:3072], c_cB)
    slot_guard = [None, (c_dv, c_dv.n)]
    slot_guard_pl = [None, (c_ac, c_ac.n)]
    pro_fo_g = (c_fo, c_fo.n)
    stages = [("qkv", qkvT_s, E3, KD), ("sw", swT_s, ESW, KD), ("out", outT_s, D, KD),
              ("gate", wgT_s, DFF, KD), ("up", wuT_s, DFF, KD), ("down", wdT_s, D, KF)]
    halves = []
    piece_map = {"qkv": (4, 4), "sw": (4, 4), "out": (4, 4),
                 "gate": (4, 4), "up": (4, 4), "down": (11, 11)}
    for t in range(NT):
        for (sname, ws, width, kc) in stages:
            k0 = 0
            for nk in piece_map[sname]:
                halves.append((ws, width, k0, nk))
                k0 += nk

    wstate = {"next": 0, "rel": [[pro_fo_g], [pro_fo_g]], "ld": {}}
    wsem = [c_wA, c_wB]

    def load_next_half():
        idx = wstate["next"]
        if idx >= len(halves):
            return
        ws, width, k0, nk = halves[idx]
        bi = idx % 2
        buf = wbufA if bi == 0 else wbufB
        wts = [(wsem[bi], wsem[bi].n)] + list(wstate["rel"][bi])
        if idx < 12 and ws in (wgT_s, wuT_s, wdT_s):
            wts.append(pro_fo_g)
        dma(S, wsem[bi], buf.ap()[:, 0:nk * width].rearrange("p (k w) -> p k w", w=width),
            ws.ap()[k0 * 128:(k0 + nk) * 128, :].rearrange("(k p) w -> p k w", p=128),
            waits=wts)
        wstate["ld"][idx] = (wsem[bi].n, buf, k0, nk, width)
        wstate["next"] = idx + 1

    def use_half(idx):
        ld, buf, k0, nk, width = wstate["ld"][idx]
        return (wsem[idx % 2], ld), buf.ap()[:, 0:nk * width].rearrange("p (k w) -> p k w", w=width), k0, nk

    def release_half(idx):
        wstate["rel"][idx % 2] = [(c_pe, c_pe.n)]
        load_next_half()

    load_next_half()
    load_next_half()
    hidx = [0]

    def stage_pieces(n):
        idxs = list(range(hidx[0], hidx[0] + n))
        hidx[0] += n
        infos = [use_half(i) for i in idxs]
        return idxs, infos

    def piece_for_k(infos, k):
        for gd, vw, k0, nk in infos:
            if k0 <= k < k0 + nk:
                return gd, vw, k0
        raise KeyError(k)

    def k_waits(infos, k, first_extra):
        wt = []
        if k == 0:
            wt += list(first_extra)
        for gd, vw, k0, nk in infos:
            if k == k0:
                wt.append(gd)
        return wt

    x_guard = [None]
    out_guard = [None]
    carry_ac_g = [None]
    sig_done = [None]
    kv_done = [None]

    def mm_stage(sname, lhs_fn, widths, evac_order, evac_fn, dep, KC):
        idxs, infos = stage_pieces(len(piece_map[sname]))
        ncf = len(widths)
        for j in range(NJ):
            banks = {}
            for ci_, c in enumerate(evac_order):
                n0, n1 = widths[c]
                pb, b, bw = next_bank(n1 - n0)
                banks[c] = (pb, b)
                for k in range(KC):
                    gd, wv, koff = piece_for_k(infos, k)
                    wt = list(bw) if k == 0 else []
                    if k == 0 and ci_ == 0:
                        wt.append(dep)
                    for gd2, _, k0_, _ in infos:
                        if k == k0_:
                            wt.append(gd2)
                    mm(pb, lhs_fn(k, j), wv[:, k - koff, n0:n1],
                       start=(k == 0), stop=(k == KC - 1), waits=wt,
                       incs=[(c_pe, 1)] if k == KC - 1 else ())
                pe_val = c_pe.n
                evac_fn(j, c, pb, b, pe_val)
        for i_ in idxs:
            release_half(i_)

    def emit_tile(t):
        jr = range(NJ)
        if t == 0:
            wx0 = [g for g in (slot_guard[1], slot_guard_pl[1]) if g is not None]
            dma(Y, c_x, x_v,
                x_h.ap()[0:TT, :].rearrange("(j p) d -> p j d", p=128),
                waits=wx0)
        xg = x_guard[0] if x_guard[0] is not None else (c_x, c_x.n)

        vreduce(s1r.ap(), x_v, A.add, waits=[xg, (c_dv, None)], incs=[(c_dv, 1)])
        vts(negm.ap(), s1r.ap(), -1.0 / 1024.0, None, A.mult,
            waits=[(c_dv, None)], incs=[(c_dv, 1)])
        negm_dv = c_dv.n
        junk = [scr1.ap()[:, 0:1024], scr1.ap()[:, 1024:2048],
                scr2.ap()[:, 0:1024], scr2.ap()[:, 1024:2048]]
        for j in jr:
            vts(zc_v[:, j, :], x_v[:, j, :], negm.ap()[:, j:j + 1], None, A.add,
                waits=[(c_dv, negm_dv)], incs=[(c_dv, 1)])
            act(AF.Square, junk[j % 4], x_v[:, j, :],
                bias=negm.ap()[:, j:j + 1], accum=s2r.ap()[:, j:j + 1],
                waits=[(c_dv, negm_dv), (c_ac, None)], incs=[(c_ac, 1)])
        zc_ac = c_ac.n
        zc_dv = c_dv.n
        act(AF.Ln, tmpr.ap(), s2r.ap(), bias=c_lneps.ap(), scale=c_i1024.ap(),
            waits=[(c_ac, None)], incs=[(c_ac, 1)])
        act(AF.Exp, r_r.ap(), tmpr.ap(), scale=c_nhalf.ap(),
            waits=[(c_ac, None)], incs=[(c_ac, 1)])
        act(AF.Square, tmpr2.ap(), r_r.ap(), waits=[(c_ac, None)], incs=[(c_ac, 1)])
        act(AF.Ln, tmpr.ap(), tmpr2.ap(), bias=c_1p.ap(), scale=c_m1em5.ap(),
            waits=[(c_ac, None)], incs=[(c_ac, 1)])
        act(AF.Exp, rr_r.ap(), tmpr.ap(), scale=c_nhalf.ap(),
            waits=[(c_ac, None)], incs=[(c_ac, 1)])
        rows_ac = c_ac.n
        vtt(alpha.ap(), r_r.ap(), rr_r.ap(), A.mult,
            waits=[(c_ac, rows_ac), (c_dv, None)], incs=[(c_dv, 1)])
        vts(gamma.ap(), r_r.ap(), 1.0, None, A.add, incs=[(c_dv, 1)])
        vtt(beta.ap(), negm.ap(), r_r.ap(), A.mult, incs=[(c_dv, 1)])
        rows_dv = c_dv.n
        for j in jr:
            wt = [(c_dv, rows_dv)]
            if out_guard[0] is not None:
                wt.append(out_guard[0])
            vts(acc_v[:, j, :], x_v[:, j, :],
                gamma.ap()[:, j:j + 1], beta.ap()[:, j:j + 1], A.mult, A.add,
                waits=wt, incs=[(c_dv, 1)])
        acc_dv = c_dv.n
        for j in jr:
            for g2 in range(2):
                pb, bnk, bw = next_bank(512)
                for fi in range(4):
                    fc = g2 * 4 + fi
                    wt = []
                    if fi == 0:
                        wt = [(c_dv, max(zc_dv, consts_dv))] + list(bw)
                    p.step(Tn, lambda e, o=pb[:, fi * 128:(fi + 1) * 128],
                           s_=zc_v[:, j, fc * 128:(fc + 1) * 128]:
                           e.transpose(o, s_, idm_f.ap()),
                           waits=wt, incs=[(c_pe, 1)] if fi == 3 else ())
                act(AF.Copy, fmD[:, g2 * 4:g2 * 4 + 4, j * 128:(j + 1) * 128],
                    pb.rearrange("p (k w) -> p k w", w=128)[:, 0:4, :],
                    waits=[(c_pe, None), (c_ac, None)], incs=[(c_ac, 1)])
                bank7[bnk] = (c_ac, c_ac.n)
        zcT_g = (c_ac, c_ac.n)

        qtmp = scr4.ap()[:, 1024:2048]
        ktmp = scr4.ap()[:, 0:1024]
        vtmp = scr3.ap()[:, 1024:2048]
        lnw = scr2.ap()[:, 0:1024]
        rinv = scr2.ap()[:, 1024:2048]
        idxs_q, infos_q = stage_pieces(len(piece_map["qkv"]))
        rot[0] = 0
        rot_mod[0] = 3
        yT_gs = []
        kv_vals = {}

        def qkv_grp(j):
            for c in range(6):
                pb, bq, bw = next_bank()
                for k in range(KD):
                    gd, wv, koff = piece_for_k(infos_q, k)
                    wt = list(bw) if k == 0 else []
                    if k == 0 and c == 0 and j == 0:
                        wt.append(zcT_g)
                    for gd2, _, k0_, _ in infos_q:
                        if k == k0_:
                            wt.append(gd2)
                    mm(pb, fmD[:, k, j * 128:(j + 1) * 128],
                       wv[:, k - koff, c * 512:(c + 1) * 512],
                       start=(k == 0), stop=(k == KD - 1),
                       waits=wt, incs=[(c_pe, 1)] if k == KD - 1 else ())
                pe_val = c_pe.n
                if c < 2:
                    dstp = qtmp[:, c * 512:(c + 1) * 512]
                elif c < 4:
                    dstp = ktmp[:, (c - 2) * 512:(c - 1) * 512]
                else:
                    dstp = vtmp[:, (c - 4) * 512:(c - 3) * 512]
                wt0 = [(c_pe, pe_val), (c_ac, None), (c_dv, rows_dv)]
                if c == 4:
                    wt0.append((c_cr, None))
                if c == 0 and sig_done[0] is not None:
                    wt0.append(sig_done[0])
                act(AF.Copy, dstp, pb, scale=alpha.ap()[:, j:j + 1],
                    waits=wt0, incs=[(c_ac, 1)])
                bank7[bq] = (c_ac, c_ac.n)
            act(AF.Square, scr3.ap()[:, 0:1024], qtmp, accum=qs2.ap()[:, j:j + 1],
                waits=[(c_ac, None)], incs=[(c_ac, 1)])
            act(AF.Square, scr3.ap()[:, 0:1024], ktmp, accum=ks2.ap()[:, j:j + 1],
                waits=[(c_ac, None)], incs=[(c_ac, 1)])
            act(AF.Ln, tmpr.ap()[:, j:j + 1], qs2.ap()[:, j:j + 1],
                bias=c_rmseps.ap(), scale=c_i1024.ap(),
                waits=[(c_ac, None)], incs=[(c_ac, 1)])
            act(AF.Ln, tmpr2.ap()[:, j:j + 1], ks2.ap()[:, j:j + 1],
                bias=c_rmseps.ap(), scale=c_i1024.ap(), incs=[(c_ac, 1)])
            act(AF.Exp, rq_r.ap()[:, j:j + 1], tmpr.ap()[:, j:j + 1],
                scale=c_nhalf.ap(), waits=[(c_ac, None)], incs=[(c_ac, 1)])
            act(AF.Exp, rk_r.ap()[:, j:j + 1], tmpr2.ap()[:, j:j + 1],
                scale=c_nhalf.ap(), incs=[(c_ac, 1)])
            act(AF.Sigmoid, sigq[:, j, :], qtmp, scale=rq_r.ap()[:, j:j + 1],
                waits=[(c_ac, None)], incs=[(c_ac, 1)])
            act(AF.Exp, wkv[:, j, 0:1024], ktmp, scale=rk_r.ap()[:, j:j + 1],
                incs=[(c_ac, 1)])
            vtt(wkv[:, j, 1024:2048], wkv[:, j, 0:1024], vtmp[:, 0:1024], A.mult,
                waits=[(c_ac, None), (c_dv, None)], incs=[(c_dv, 1)])
            sig_done[0] = (c_dv, c_dv.n)
            kv_vals[j] = c_dv.n

        def cumy_grp(j):
            gj = t * NJ + j
            cin = carry[gj % 2]
            cout = carry[(gj + 1) % 2]
            for q4 in range(4):
                pb = pall.ap()[:, q4 * 512:(q4 + 1) * 512]
                wt = []
                if bank7[q4] is not None:
                    wt.append(bank7[q4])
                    bank7[q4] = None
                if q4 == 0:
                    wt += [(c_dv, kv_vals[j])]
                    if carry_ac_g[0] is not None:
                        wt.append(carry_ac_g[0])
                mm(pb, tri_b.ap(), wkv[:, j, q4 * 512:(q4 + 1) * 512],
                   start=True, stop=False, waits=wt)
                mm(pb, ones1_b.ap(), cin[0:1, q4 * 512:(q4 + 1) * 512],
                   start=False, stop=True, incs=[(c_pe, 1)] if q4 == 3 else ())
            cums_pe = c_pe.n
            act(AF.Copy, scr3.ap()[0:64, 0:2048], pall.ap()[64:128, 0:2048],
                waits=[(c_pe, cums_pe), (c_ac, None), (c_dv, None)], incs=[(c_ac, 1)])
            dma(S, c_cr, cout, scr3.ap()[63:64, 0:2048],
                waits=[(c_cr, c_cr.n), (c_ac, None)])
            carry_ac_g[0] = (c_cr, c_cr.n)
            act(AF.Ln, lnw, pall.ap()[:, 0:1024], bias=c_weps.ap(),
                waits=[(c_ac, None)], incs=[(c_ac, 1)])
            vtt(y_v[:, j, :], sigq[:, j, :], pall.ap()[:, 1024:2048], A.mult,
                waits=[(c_pe, cums_pe), (c_dv, None)], incs=[(c_dv, 1)])
            act(AF.Exp, rinv, lnw, scale=c_neg1.ap(),
                waits=[(c_ac, None), (c_dv, None)], incs=[(c_ac, 1)])
            vtt(y_v[:, j, :], y_v[:, j, :], rinv, A.mult,
                waits=[(c_ac, None), (c_dv, None)], incs=[(c_dv, 1)])
            yev_dv = c_dv.n
            for q4 in range(4):
                bank7[q4] = (c_dv, yev_dv)
            transpose_group(
                [y_v[:, j, fc * 128:(fc + 1) * 128] for fc in range(KD)],
                fmD[:, :, j * 128:(j + 1) * 128],
                deps=[(c_dv, yev_dv)])
            yT_gs.append((c_ac, c_ac.n))

        sw_state = {}

        def sw_grp(j):
            if "pieces" not in sw_state:
                sw_state["pieces"] = stage_pieces(len(piece_map["sw"]))
            idxs_sw, infos_sw = sw_state["pieces"]
            sgt = scr1.ap()[:, (j % 2) * 1024:(j % 2) * 1024 + 1024]
            for ci_, c in enumerate((2, 3, 0, 1)):
                pb, bq, bw = next_bank()
                for k in range(KD):
                    gd, wv, koff = piece_for_k(infos_sw, k)
                    wt = list(bw) if k == 0 else []
                    if k == 0 and ci_ == 0:
                        wt.append(yT_gs[j])
                    for gd2, _, k0_, _ in infos_sw:
                        if k == k0_:
                            wt.append(gd2)
                    mm(pb, fmD[:, k, j * 128:(j + 1) * 128],
                       wv[:, k - koff, c * 512:(c + 1) * 512],
                       start=(k == 0), stop=(k == KD - 1), waits=wt,
                       incs=[(c_pe, 1)] if k == KD - 1 else ())
                if c >= 2:
                    act(AF.Silu, sgt[:, (c - 2) * 512:(c - 1) * 512], pb,
                        waits=[(c_pe, None), (c_ac, None), (c_dv, None)], incs=[(c_ac, 1)])
                    bank7[bq] = (c_ac, c_ac.n)
                else:
                    vtt(s_v[:, j, c * 512:(c + 1) * 512],
                        sgt[:, c * 512:(c + 1) * 512], pb, A.mult,
                        waits=[(c_pe, None), (c_ac, None), (c_dv, None)], incs=[(c_dv, 1)])
                    bank7[bq] = (c_dv, c_dv.n)
            transpose_group(
                [s_v[:, j, fc * 128:(fc + 1) * 128] for fc in range(KD)],
                fmD[:, :, j * 128:(j + 1) * 128],
                deps=[(c_dv, None)])

        if NJ == 4:
            qkv_grp(0)
            qkv_grp(1)
            cumy_grp(0)
            qkv_grp(2)
            cumy_grp(1)
            qkv_grp(3)
            cumy_grp(2)
            cumy_grp(3)
            for i_ in idxs_q:
                release_half(i_)
            sw_grp(0)
            sw_grp(1)
            sw_grp(2)
            sw_grp(3)
        else:
            for j in jr:
                qkv_grp(j)
            for j in jr:
                cumy_grp(j)
            for i_ in idxs_q:
                release_half(i_)
            for j in jr:
                sw_grp(j)
        rot[0] = 0
        rot_mod[0] = 6
        for i_ in sw_state["pieces"][0]:
            release_half(i_)
        sT_g = (c_ac, c_ac.n)

        xq_sets = []
        for par in range(2):
            o = par * 1024
            xq_sets.append(dict(
                set_id=par,
                vp=vp32.ap()[:, o:o + 1024],
                u1=scr1.ap()[:, o:o + 1024], z1=scr3.ap()[:, o:o + 1024],
                q2=scr2.ap()[:, o:o + 1024],
                sg=scr4.ap()[:, 1024:2048] if par == 0 else x_tm.ap()[:, 0:1024],
                bm=bmrow.ap()[:, par * 64:par * 64 + 64],
                sc=scrow.ap()[:, par * 64:par * 64 + 64],
                sh=shrow.ap()[:, par * 64:par * 64 + 64],
                rs=rsrow.ap()[:, par * 64:par * 64 + 64],
                vp16=vpx.ap()[:, o:o + 1024]))
        quant_multi([(zc_v[:, j2, :], xq_v[:, j2, :], 1024, xq_sets[j2 % 2])
                     for j2 in range(min(2, NJ))],
                    waits=[(c_ac, None), (c_dv, None), (c_cr, None),
                           (c_pl, None), (c_pe, None)])
        xq_g01 = [(c_dv, c_dv.n), (c_pl, c_pl.n)]

        def out_evac(j, c, pb, b, pe_val):
            vtt(acc_v[:, j, c * 512:(c + 1) * 512],
                acc_v[:, j, c * 512:(c + 1) * 512], pb, A.add,
                waits=[(c_pe, pe_val), (c_dv, acc_dv)], incs=[(c_dv, 1)])
            bank7[b] = (c_dv, c_dv.n)

        mm_stage("out", lambda k, j: fmD[:, k, j * 128:(j + 1) * 128],
                 [(0, 512), (512, 1024)], (0, 1), out_evac, sT_g, KD)
        o_dv = c_dv.n
        if NJ > 2:
            quant_multi([(zc_v[:, j2, :], xq_v[:, j2, :], 1024, xq_sets[j2 % 2])
                         for j2 in range(2, NJ)],
                        waits=[(c_ac, None)])
        xq_g23 = [(c_dv, c_dv.n), (c_pl, c_pl.n)]
        xqc_dv = c_dv.n
        xqc_pl = c_pl.n
        if t + 1 < NT:
            dma(Y, c_x, x_v,
                x_h.ap()[(t + 1) * TT:(t + 2) * TT, :].rearrange("(j p) d -> p j d", p=128),
                waits=[(c_x, c_x.n), (c_ac, None), (c_dv, xqc_dv), (c_pl, xqc_pl)])
            x_guard[0] = (c_x, c_x.n)

        NCF = (DFF + 511) // 512
        widthsF = [(c * 512, min((c + 1) * 512, DFF)) for c in range(NCF)]
        idxs_g, infos_g = stage_pieces(len(piece_map["gate"]))
        xqT_js = []
        for j in jr:
            transpose_group(
                [xq_v[:, j, fc * 128:(fc + 1) * 128] for fc in range(KD)],
                fmD[:, :, j * 128:(j + 1) * 128],
                deps=(xq_g01 if j < 2 else xq_g23) + [(c_dv, o_dv)])
            xqT_js.append((c_ac, c_ac.n))
            sgate = hq_v[:, j, :]
            for c in range(NCF):
                n0, n1 = widthsF[c]
                pb, bq, bw = next_bank(n1 - n0)
                for k in range(KD):
                    gd, wv, koff = piece_for_k(infos_g, k)
                    wt = list(bw) if k == 0 else []
                    if k == 0 and c == 0:
                        wt.append(xqT_js[j])
                    for gd2, _, k0_, _ in infos_g:
                        if k == k0_:
                            wt.append(gd2)
                    mm(pb, fmD[:, k, j * 128:(j + 1) * 128], wv[:, k - koff, n0:n1],
                       start=(k == 0), stop=(k == KD - 1), waits=wt,
                       incs=[(c_pe, 1)] if k == KD - 1 else ())
                act(AF.Silu, sgate[:, n0:n1], pb, scale=r_r.ap()[:, j:j + 1],
                    waits=[(c_pe, None), (c_ac, None), (c_dv, None)],
                    incs=[(c_ac, 1)])
                bank7[bq] = (c_ac, c_ac.n)
        for i_ in idxs_g:
            release_half(i_)
        xqT_g = (c_ac, c_ac.n)

        hq_sets = []
        for par in range(2):
            o = par * 1408
            hq_sets.append(dict(
                set_id=par,
                vp=zc.ap()[:, o:o + 1408],
                u1=u1q.ap()[:, o:o + 1408],
                z1=(scr2 if par == 0 else scr3).ap()[:, 0:1408],
                q2=aliasB.ap()[:, o:o + 1408], sg=aliasA.ap()[:, o:o + 1408],
                bm=bmrow.ap()[:, par * 88:par * 88 + 88],
                sc=scrow.ap()[:, par * 88:par * 88 + 88],
                sh=shrow.ap()[:, par * 88:par * 88 + 88],
                rs=rsrow.ap()[:, par * 88:par * 88 + 88],
                vp16=(scr1 if par == 0 else vpx).ap()[:, 0:1408]))
        idxs, infos = stage_pieces(len(piece_map["up"]))
        chain_gs = []
        for j in jr:
            sgate = hq_v[:, j, :]
            for c in range(NCF):
                n0, n1 = widthsF[c]
                pb, b, bw = next_bank(n1 - n0)
                for k in range(KD):
                    gd, wv, koff = piece_for_k(infos, k)
                    wt = list(bw) if k == 0 else []
                    if k == 0 and c == 0 and j == 0:
                        wt.append(xqT_g)
                    for gd2, _, k0_, _ in infos:
                        if k == k0_:
                            wt.append(gd2)
                    mm(pb, fmD[:, k, j * 128:(j + 1) * 128], wv[:, k - koff, n0:n1],
                       start=(k == 0), stop=(k == KD - 1), waits=wt,
                       incs=[(c_pe, 1)] if k == KD - 1 else ())
                vsts(h_v[:, j, n0:n1], pb, r_r.ap()[:, j:j + 1],
                     sgate[:, n0:n1], A.mult, A.mult,
                     waits=[(c_pe, None), (c_ac, None), (c_dv, None)],
                     incs=[(c_dv, 1)])
                bank7[b] = (c_dv, c_dv.n)
            hj_dv = c_dv.n
            quant_multi([(h_v[:, j, hh * 1408:(hh + 1) * 1408],
                          hq_v[:, j, hh * 1408:(hh + 1) * 1408],
                          1408, hq_sets[hh]) for hh in range(2)],
                        waits=[(c_dv, hj_dv), (c_ac, None)])
            chain_gs.append([(c_dv, c_dv.n), (c_pl, c_pl.n)])
        for i_ in idxs:
            release_half(i_)

        idxs, infos = stage_pieces(len(piece_map["down"]))
        for j in jr:
            for g in range(3):
                f0, f1 = g * 8, min(g * 8 + 8, KF)
                transpose_group(
                    [hq_v[:, j, fc * 128:(fc + 1) * 128] for fc in range(f0, f1)],
                    hqT2[:, j, f0:f1, :],
                    deps=chain_gs[j])
            hqT_j = (c_ac, c_ac.n)
            for c in range(2):
                pb, b, bw = next_bank()
                for k in range(KF):
                    gd, wv, koff = piece_for_k(infos, k)
                    wt = list(bw) if k == 0 else []
                    if k == 0:
                        wt.append(hqT_j)
                    for gd2, _, k0_, _ in infos:
                        if k == k0_:
                            wt.append(gd2)
                    mm(pb, hqT2[:, j, k, :], wv[:, k - koff, c * 512:(c + 1) * 512],
                       start=(k == 0), stop=(k == KF - 1), waits=wt,
                       incs=[(c_pe, 1)] if k == KF - 1 else ())
                wt2 = [(c_pe, None), (c_dv, o_dv)]
                if out_guard[0] is not None:
                    wt2.append(out_guard[0])
                vtt(acc_v[:, j, c * 512:(c + 1) * 512],
                    acc_v[:, j, c * 512:(c + 1) * 512], pb, A.add,
                    waits=wt2, incs=[(c_dv, 1)])
                bank7[b] = (c_dv, c_dv.n)
        for i_ in idxs:
            release_half(i_)
        dma(Y, c_out, out_h.ap()[t * TT:(t + 1) * TT, :].rearrange("(j p) d -> p j d", p=128),
            acc.ap()[:, 0:NJ * 1024].rearrange("p (j w) -> p j w", w=1024),
            waits=[(c_out, c_out.n), (c_dv, None)])
        out_guard[0] = (c_out, c_out.n)

    for t in range(NT):
        emit_tile(t)

    p.step(G, lambda e: e.nop(), waits=[(c_out, None)])

    with nc.Block() as block:
        @block.sync
        def _(e):
            for f in p.q["sync"]:
                f(e)

        @block.scalar
        def _(e):
            for f in p.q["scalar"]:
                f(e)

        @block.vector
        def _(e):
            for f in p.q["vector"]:
                f(e)

        @block.tensor
        def _(e):
            for f in p.q["tensor"]:
                f(e)

        @block.gpsimd
        def _(e):
            for f in p.q["gpsimd"]:
                f(e)

    es.close()
    return nc


_NC_CACHE = {}


def kernel(L0_attn_x, L0_attn_w_qkv, L0_attn_w_swiglu, L0_attn_w_out,
           L0_ffn_w_gate, L0_ffn_w_up, L0_ffn_w_down):
    x = np.ascontiguousarray(np.asarray(L0_attn_x, dtype=np.float32))
    B, T, _ = x.shape
    key = (T,)
    if key not in _NC_CACHE:
        _NC_CACHE[key] = build(TT=512, NT=T // 512)
    nc = _NC_CACHE[key]
    wqkvT = np.ascontiguousarray(np.asarray(L0_attn_w_qkv, dtype=np.float32).T)
    wswT = np.ascontiguousarray(np.asarray(L0_attn_w_swiglu, dtype=np.float32).T)
    woutT = np.ascontiguousarray(np.asarray(L0_attn_w_out, dtype=np.float32).T)
    wg = np.ascontiguousarray(np.asarray(L0_ffn_w_gate, dtype=np.float32))
    wu = np.ascontiguousarray(np.asarray(L0_ffn_w_up, dtype=np.float32))
    wd = np.ascontiguousarray(np.asarray(L0_ffn_w_down, dtype=np.float32))
    tri = np.triu(np.ones((128, 128), dtype=np.float32))
    idm = np.eye(128, dtype=np.float32)
    ones1 = np.ones((1, 128), dtype=np.float32)
    ones128 = np.ones((128, 1), dtype=np.float32)
    in_maps = []
    for b in range(B):
        in_maps.append({
            "x": np.ascontiguousarray(x[b]),
            "wqkvT": wqkvT, "wswT": wswT, "woutT": woutT,
            "wg": wg, "wu": wu, "wd": wd,
            "tri": tri, "idm": idm, "ones1": ones1, "ones128": ones128,
        })
    res = run_bass_kernel_spmd(nc, in_maps, core_ids=list(range(B)))
    out = np.stack([np.asarray(res.results[b]["out"]) for b in range(B)], axis=0)
    return out.astype(np.float32)


# version 30
# speedup vs baseline: 1.0996x; 1.0402x over previous
import numpy as np
from contextlib import ExitStack

import concourse.bass as bass
import concourse.mybir as mybir
from concourse.bass_utils import run_bass_kernel_spmd

F32 = mybir.dt.float32
BF16 = mybir.dt.bfloat16
F16 = mybir.dt.float16
A = mybir.AluOpType
AF = mybir.ActivationFunctionType
AX = mybir.AxisListType

D = 1024
E3 = 3072
ESW = 2048
DFF = 2816
KD = D // 128
KF = DFF // 128
LN_EPS = 1e-5
RMS_EPS = 1.1920929e-07
W_EPS = 1e-6

ENGS = ("sync", "scalar", "vector", "tensor", "gpsimd")
S, Y, V, Tn, G = ENGS


class Ctr:
    def __init__(self, nc, es, name, step=1):
        self.sem = es.enter_context(nc.semaphore(name))
        self.n = 0
        self.step = step

    def bump(self, k=1):
        self.n += k * self.step
        return self.n


class Plan:
    def __init__(self, nc):
        self.nc = nc
        self.q = {e: [] for e in ENGS}
        self.last_wait = {e: {} for e in ENGS}

    def step(self, eng, mk, waits=(), incs=()):
        wl = []
        for c, v in waits:
            val = c.n if v is None else v
            if val <= 0:
                continue
            if self.last_wait[eng].get(c.sem.name, -1) >= val:
                continue
            self.last_wait[eng][c.sem.name] = val
            wl.append((c.sem, val))
        il = []
        for c, k in incs:
            il.append((c.sem, k * c.step))
            c.bump(k)

        def f(e, wl=wl, il=il, mk=mk):
            for s_, v_ in wl:
                e.wait_ge(s_, v_)
            inst = mk(e)
            for s_, v_ in il:
                inst.then_inc(s_, v_)
        self.q[eng].append(f)


def build(TT=512, NT=8):
    T = TT * NT
    NJ = TT // 128
    nc = bass.Bass("TRN2", target_bir_lowering=False)

    x_h = nc.dram_tensor("x", [T, D], F32, kind="ExternalInput")
    wqkvT_h = nc.dram_tensor("wqkvT", [D, E3], F32, kind="ExternalInput")
    wswT_h = nc.dram_tensor("wswT", [D, ESW], F32, kind="ExternalInput")
    woutT_h = nc.dram_tensor("woutT", [D, D], F32, kind="ExternalInput")
    wg_h = nc.dram_tensor("wg", [DFF, D], F32, kind="ExternalInput")
    wu_h = nc.dram_tensor("wu", [DFF, D], F32, kind="ExternalInput")
    wd_h = nc.dram_tensor("wd", [D, DFF], F32, kind="ExternalInput")
    tri_h = nc.dram_tensor("tri", [128, 128], F32, kind="ExternalInput")
    idm_h = nc.dram_tensor("idm", [128, 128], F32, kind="ExternalInput")
    ones1_h = nc.dram_tensor("ones1", [1, 128], F32, kind="ExternalInput")
    ones128_h = nc.dram_tensor("ones128", [128, 1], F32, kind="ExternalInput")
    out_h = nc.dram_tensor("out", [T, D], F32, kind="ExternalOutput")
    qkvT_s = nc.dram_tensor("qkvT_s", [D, E3], BF16, kind="Internal")
    swT_s = nc.dram_tensor("swT_s", [D, ESW], BF16, kind="Internal")
    outT_s = nc.dram_tensor("outT_s", [D, D], BF16, kind="Internal")
    wgT_s = nc.dram_tensor("wgT_s", [D, DFF], BF16, kind="Internal")
    wuT_s = nc.dram_tensor("wuT_s", [D, DFF], BF16, kind="Internal")
    wdT_s = nc.dram_tensor("wdT_s", [DFF, D], BF16, kind="Internal")

    es = ExitStack()
    sb = lambda n, s_, d: es.enter_context(nc.sbuf_tensor(n, s_, d))
    psb = lambda n, s_, d: es.enter_context(nc.psum_tensor(n, s_, d))

    x_tm = sb("x_tm", [128, max(NJ * 1024, 4096)], F32)
    acc = sb("acc", [128, max(NJ * 1024, 4096)], F32)
    vp32 = sb("vp32", [128, 2048], F32)
    u1q = sb("u1q", [128, 2816], BF16)
    zc = sb("zc", [128, max(NJ * 1024, 2816)], F32)
    arena24 = sb("arena24", [128, max(NJ * 3072, 12288)], BF16)
    arenaQ = sb("arenaQ", [128, max(NJ * 2816, 8192)], BF16)
    aliasA = sb("aliasA", [128, max(NJ * 1024, 4096)], BF16)
    aliasB = sb("aliasB", [128, max(NJ * 1024, 4096)], BF16)
    aliasD = sb("aliasD", [128, max(KD * TT, 4096)], BF16)
    scr1 = sb("scr1", [128, 2048], F16)
    vpx = sb("vpx", [128, 2048], F16)
    scr2 = sb("scr2", [128, 2048], BF16)
    scr3 = sb("scr3", [128, 2048], BF16)
    scr4 = sb("scr4", [128, 2048], BF16)
    wbufA = sb("wbufA", [128, 12288], BF16)
    wbufB = sb("wbufB", [128, 12288], BF16)
    tri_b = sb("tri_b", [128, 128], BF16)
    idm_b = sb("idm_b", [128, 128], BF16)
    idm_f = sb("idm_f", [128, 128], F32)
    ones1_b = sb("ones1_b", [1, 128], BF16)
    ones128_b = sb("ones128_b", [128, 1], BF16)
    carryT = sb("carryT", [1, 2048], BF16)
    carry = [carryT.ap()[:, 0:2048], carryT.ap()[:, 0:2048]]
    s1r = sb("s1r", [128, NJ], F32)
    s2r = sb("s2r", [128, NJ], F32)
    r_r = sb("r_r", [128, NJ], F32)
    rr_r = sb("rr_r", [128, NJ], F32)
    alpha = sb("alpha", [128, NJ], F32)
    gamma = sb("gamma", [128, NJ], F32)
    beta = sb("beta", [128, NJ], F32)
    tmpr = sb("tmpr", [128, NJ], F32)
    tmpr2 = sb("tmpr2", [128, NJ], F32)
    qs2 = sb("qs2", [128, NJ], F32)
    ks2 = sb("ks2", [128, NJ], F32)
    rq_r = sb("rq_r", [128, NJ], F32)
    rk_r = sb("rk_r", [128, NJ], F32)
    bmrow = sb("bmrow", [128, 176], F32)
    scrow = bmrow
    shrow = sb("shrow", [128, 176], F32)
    rsrow = sb("rsrow", [128, 176], F32)
    negm = sb("negm", [128, NJ], F32)
    c_lneps = sb("c_lneps", [128, 1], F32)
    c_rmseps = sb("c_rmseps", [128, 1], F32)
    c_weps = sb("c_weps", [128, 1], F32)
    c_i1024 = sb("c_i1024", [128, 1], F32)
    c_neg1 = sb("c_neg1", [128, 1], F32)
    c_nhalf = sb("c_nhalf", [128, 1], F32)
    c_m1em5 = sb("c_m1em5", [128, 1], F32)
    c_1p = sb("c_1p", [128, 1], F32)
    c_m25 = sb("c_m25", [128, 1], F32)
    c_m35 = sb("c_m35", [128, 1], F32)
    c_m50 = sb("c_m50", [128, 1], F32)

    pall = psb("pall", [128, 3584], F32)
    pt = psb("pt", [128, 1024], BF16)

    p = Plan(nc)
    c_wA = Ctr(nc, es, "c_wA", 16)
    c_wB = Ctr(nc, es, "c_wB", 16)
    c_x = Ctr(nc, es, "c_x", 16)
    c_out = Ctr(nc, es, "c_out", 16)
    c_cst = Ctr(nc, es, "c_cst", 16)
    c_pA = Ctr(nc, es, "c_pA", 16)
    c_pB = Ctr(nc, es, "c_pB", 16)
    c_cA = Ctr(nc, es, "c_cA", 16)
    c_cB = Ctr(nc, es, "c_cB", 16)
    c_fo = Ctr(nc, es, "c_fo", 16)
    c_pe = Ctr(nc, es, "c_pe")
    c_dv = Ctr(nc, es, "c_dv")
    c_ac = Ctr(nc, es, "c_ac")
    c_pl = Ctr(nc, es, "c_pl")
    c_cr = Ctr(nc, es, "c_cr", 16)

    qkv_sb = arena24.ap()[:, 0:NJ * 3072].rearrange("p (j w) -> p j w", w=3072)
    h_v = arena24.ap()[:, 0:NJ * 2816].rearrange("p (j w) -> p j w", w=2816)
    hqT_v = arena24.ap()[:, 0:KF * TT].rearrange("p (k w) -> p k w", w=TT)
    hqT2 = arena24.ap()[:, 0:NJ * KF * 128].rearrange("p (j k w) -> p j k w", k=KF, w=128)
    wkv = arenaQ.ap()[:, 0:NJ * 2048].rearrange("p (j w) -> p j w", w=2048)
    hq_v = arenaQ.ap()[:, 0:NJ * 2816].rearrange("p (j w) -> p j w", w=2816)
    sigq = aliasA.ap()[:, 0:NJ * 1024].rearrange("p (j w) -> p j w", w=1024)
    xq_v = sigq
    y_v = aliasB.ap()[:, 0:NJ * 1024].rearrange("p (j w) -> p j w", w=1024)
    s_v = y_v
    fmD = aliasD.ap()[:, 0:KD * TT].rearrange("p (k w) -> p k w", w=TT)
    zc_v = zc.ap()[:, 0:NJ * 1024].rearrange("p (j w) -> p j w", w=1024)
    x_v = x_tm.ap()[:, 0:NJ * 1024].rearrange("p (j w) -> p j w", w=1024)
    acc_v = acc.ap()[:, 0:NJ * 1024].rearrange("p (j w) -> p j w", w=1024)

    def dma(eng, ctr, out_ap, in_ap, waits=()):
        p.step(eng, lambda e, o=out_ap, i=in_ap: e.dma_start(out=o, in_=i),
               waits=waits, incs=[(ctr, 1)])

    def mm(out_ap, lhsT_ap, rhs_ap, start, stop, waits=(), incs=()):
        p.step(Tn, lambda e, o=out_ap, l=lhsT_ap, r=rhs_ap, st=start, sp=stop:
               e.matmul(o, l, r, start=st, stop=sp), waits=waits, incs=incs)

    def act(func, out_ap, in_ap, bias=0.0, scale=1.0, accum=None, waits=(), incs=()):
        p.step(Y, lambda e, o=out_ap, i=in_ap, b=bias, s_=scale, a=accum, f=func:
               e.activation(o, i, f, bias=b, scale=s_, accum_out=a), waits=waits, incs=incs)

    def vts(out_ap, in_ap, s1_, s2_, op0, op1=None, waits=(), incs=()):
        if op1 is None:
            p.step(V, lambda e, o=out_ap, i=in_ap, a=s1_, o0=op0:
                   e.tensor_scalar(o, i, a, None, o0), waits=waits, incs=incs)
        else:
            p.step(V, lambda e, o=out_ap, i=in_ap, a=s1_, b=s2_, o0=op0, o1=op1:
                   e.tensor_scalar(o, i, a, b, o0, o1), waits=waits, incs=incs)

    def vtt(out_ap, a_ap, b_ap, op, waits=(), incs=()):
        p.step(V, lambda e, o=out_ap, a=a_ap, b=b_ap, oo=op:
               e.tensor_tensor(o, a, b, oo), waits=waits, incs=incs)

    def vsts(out_ap, in0_ap, scal, in1_ap, op0, op1, waits=(), incs=()):
        p.step(V, lambda e, o=out_ap, i0=in0_ap, q=scal, i1=in1_ap, o0=op0, o1=op1:
               e.scalar_tensor_tensor(o, i0, q, i1, o0, o1), waits=waits, incs=incs)

    def vcopy(out_ap, in_ap, waits=(), incs=()):
        p.step(V, lambda e, o=out_ap, i=in_ap: e.tensor_copy(o, i), waits=waits, incs=incs)

    def vrecip(out_ap, in_ap, waits=(), incs=()):
        p.step(V, lambda e, o=out_ap, i=in_ap: e.reciprocal(o, i), waits=waits, incs=incs)

    def vreduce(out_ap, in_ap, op, absval=False, waits=(), incs=()):
        p.step(V, lambda e, o=out_ap, i=in_ap, oo=op, av=absval: e.tensor_reduce(
            o, i, axis=AX.X, op=oo, apply_absolute_value=av) if av else e.tensor_reduce(
            o, i, axis=AX.X, op=oo), waits=waits, incs=incs)

    def ptt(out_ap, a_ap, b_ap, op, waits=(), incs=()):
        p.step(G, lambda e, o=out_ap, a=a_ap, b=b_ap, oo=op:
               e.tensor_tensor(o, a, b, oo), waits=waits, incs=incs)

    def pts(out_ap, in_ap, s1_, s2_, op0, op1=None, waits=(), incs=()):
        if op1 is None:
            p.step(G, lambda e, o=out_ap, i=in_ap, a=s1_, o0=op0:
                   e.tensor_scalar(o, i, a, None, o0), waits=waits, incs=incs)
        else:
            p.step(G, lambda e, o=out_ap, i=in_ap, a=s1_, b=s2_, o0=op0, o1=op1:
                   e.tensor_scalar(o, i, a, b, o0, o1), waits=waits, incs=incs)

    def psts(out_ap, in0_ap, scal, in1_ap, op0, op1, waits=(), incs=()):
        p.step(G, lambda e, o=out_ap, i0=in0_ap, q=scal, i1=in1_ap, o0=op0, o1=op1:
               e.scalar_tensor_tensor(o, i0, q, i1, o0, o1), waits=waits, incs=incs)

    def preduce(out_ap, in_ap, op, absval=False, waits=(), incs=()):
        p.step(G, lambda e, o=out_ap, i=in_ap, oo=op, av=absval: e.tensor_reduce(
            o, i, axis=AX.X, op=oo, apply_absolute_value=av) if av else e.tensor_reduce(
            o, i, axis=AX.X, op=oo), waits=waits, incs=incs)

    pt_guard = [None]

    def transpose_group(srcs, dst_ap, deps):
        for i, sap in enumerate(srcs):
            wt = []
            if i == 0:
                wt = list(deps)
                if pt_guard[0] is not None:
                    wt.append(pt_guard[0])
            p.step(Tn, lambda e, o=pt.ap()[:, i * 128:(i + 1) * 128], s_=sap:
                   e.transpose(o, s_, idm_b.ap()),
                   waits=wt, incs=[(c_pe, 1)] if i == len(srcs) - 1 else ())
        act(AF.Copy, dst_ap,
            pt.ap()[:, 0:len(srcs) * 128].rearrange("p (k w) -> p k w", w=128),
            waits=[(c_pe, None), (c_ac, None)], incs=[(c_ac, 1)])
        pt_guard[0] = (c_ac, c_ac.n)

    bank7 = [None] * 7
    rot = [0]
    rot_mod = [6]

    def bank_ap(b, width=512):
        return pall.ap()[:, b * 512:b * 512 + width]

    def next_bank(width=512):
        m = rot_mod[0]
        base = 4 if m == 3 else 0
        b = base + (rot[0] % m)
        rot[0] += 1
        guard = bank7[b]
        return bank_ap(b, width), b, ([] if guard is None else [guard])

    def quant_multi(jobs, waits=(), wave=2):
        for w in range(0, len(jobs), wave):
            _quant_wave(jobs[w:w + wave], waits)

    _qm_guard = {}

    def _quant_wave(jobs, waits=()):
        w0 = list(waits)
        set_guard = _qm_guard
        gs = [{} for _ in jobs]
        for i, (src_ap, dst_ap, fd, scf) in enumerate(jobs):
            gs[i]["nb"] = fd // 16
            gs[i]["sid"] = scf.get("set_id", i % 2)
        for i, (src_ap, dst_ap, fd, scf) in enumerate(jobs):
            nb = gs[i]["nb"]
            b16 = lambda ap2: ap2.rearrange("p (b k) -> p b k", k=16)
            gd = set_guard.get(gs[i]["sid"], [])
            vreduce(scf["bm"][:, 0:nb], b16(src_ap), A.max, absval=True,
                    waits=w0 + gd + [(c_dv, None), (c_pl, None)], incs=[(c_dv, 1)])
            if scf.get("ip"):
                vr = (c_dv, c_dv.n)
                p.step(Y, lambda e, o=scf["sg"], i2=src_ap: e.sign(o, i2),
                       waits=w0 + gd + [(c_ac, None)], incs=[(c_ac, 1)])
                act(AF.Abs, scf["vp"], src_ap,
                    waits=[vr, (c_ac, None)], incs=[(c_ac, 1)])
            else:
                act(AF.Abs, scf["vp"], src_ap,
                    waits=w0 + gd + [(c_ac, None)],
                    incs=[(c_ac, 1)])
                p.step(Y, lambda e, o=scf["sg"], i2=src_ap: e.sign(o, i2),
                       waits=[(c_ac, None)], incs=[(c_ac, 1)])
            gs[i]["abs"] = (c_ac, c_ac.n)
        for i, (src_ap, dst_ap, fd, scf) in enumerate(jobs):
            vts(scf["sc"], scf["bm"], 1.0 / 6.0, 1e-12, A.mult, A.max,
                waits=[(c_dv, None), (c_pl, None)], incs=[(c_dv, 1)])
            vts(scf["sh"], scf["sc"], 0.5, None, A.mult, waits=[(c_dv, None)], incs=[(c_dv, 1)])
            vrecip(scf["rs"], scf["sc"], waits=[(c_dv, None)], incs=[(c_dv, 1)])
            gs[i]["rows"] = (c_dv, c_dv.n)
        for i, (src_ap, dst_ap, fd, scf) in enumerate(jobs):
            nb = gs[i]["nb"]
            b16 = lambda ap2: ap2.rearrange("p (b k) -> p b k", k=16)
            bcr = scf["rs"].unsqueeze(2).broadcast_to([128, nb, 16])
            ptt(b16(scf["vp"]), b16(scf["vp"]), bcr, A.mult,
                waits=[gs[i]["rows"], gs[i]["abs"], (c_pl, None)], incs=[(c_pl, 1)])
            gs[i]["vp"] = (c_pl, c_pl.n)
        for i, (src_ap, dst_ap, fd, scf) in enumerate(jobs):
            vcopy(scf["vp16"], scf["vp"],
                  waits=[gs[i]["vp"], (c_dv, None), (c_pe, None), (c_ac, None)]
                  + [g for gl in set_guard.values() for g in gl],
                  incs=[(c_dv, 1)])
            gs[i]["v16"] = (c_dv, c_dv.n)
        for i, (src_ap, dst_ap, fd, scf) in enumerate(jobs):
            act(AF.Sign, scf["q2"], scf["vp"], bias=c_m25.ap(),
                waits=[gs[i]["vp"], (c_ac, None)], incs=[(c_ac, 1)])
            act(AF.Sign, scf["z1"], scf["vp"], bias=c_m35.ap(),
                incs=[(c_ac, 1)])
            gs[i]["s56"] = (c_ac, c_ac.n)
        for i, (src_ap, dst_ap, fd, scf) in enumerate(jobs):
            vtt(scf["z1"], scf["q2"], scf["z1"], A.add,
                waits=[gs[i]["s56"], (c_dv, None)], incs=[(c_dv, 1)])
            gs[i]["t1"] = (c_dv, c_dv.n)
        for i, (src_ap, dst_ap, fd, scf) in enumerate(jobs):
            act(AF.Sign, scf["q2"], scf["vp"], bias=c_m50.ap(),
                waits=[gs[i]["t1"], (c_ac, None)], incs=[(c_ac, 1)])
            gs[i]["s7"] = (c_ac, c_ac.n)
        for i, (src_ap, dst_ap, fd, scf) in enumerate(jobs):
            vsts(scf["z1"], scf["q2"], 2.0, scf["z1"], A.mult, A.add,
                 waits=[gs[i]["s7"], (c_dv, None)], incs=[(c_dv, 1)])
            vts(scf["u1"], scf["vp16"], 0.25, 4.0, A.is_gt, A.add,
                waits=[gs[i]["v16"], (c_dv, None)], incs=[(c_dv, 1)])
            for th in (0.75, 1.25, 1.75):
                vsts(scf["u1"], scf["vp16"], th, scf["u1"], A.is_gt, A.add,
                     waits=[(c_dv, None)], incs=[(c_dv, 1)])
            vtt(scf["q2"], scf["u1"], scf["z1"], A.add,
                waits=[(c_dv, None)], incs=[(c_dv, 1)])
            vtt(scf["q2"], scf["q2"], scf["sg"], A.mult,
                waits=[(c_dv, None)], incs=[(c_dv, 1)])
            gs[i]["q2g"] = (c_dv, c_dv.n)
        for i, (src_ap, dst_ap, fd, scf) in enumerate(jobs):
            nb = gs[i]["nb"]
            b16 = lambda ap2: ap2.rearrange("p (b k) -> p b k", k=16)
            bcs = scf["sh"].unsqueeze(2).broadcast_to([128, nb, 16])
            ptt(b16(dst_ap), b16(scf["q2"]), bcs, A.mult,
                waits=[gs[i]["q2g"], (c_pl, None)], incs=[(c_pl, 1)])
            set_guard[gs[i]["sid"]] = [(c_pl, c_pl.n), (c_dv, gs[i]["q2g"][1]),
                                       (c_ac, gs[i]["s7"][1])]

    def quant(src_ap, dst_ap, fd, scf, waits=()):
        quant_multi([(src_ap, dst_ap, fd, scf)], waits=waits)

    dma(Y, c_cst, x_tm.ap()[:, 0:128], tri_h.ap())
    dma(Y, c_cst, x_tm.ap()[:, 128:256], idm_h.ap())
    dma(Y, c_cst, x_tm.ap()[0:1, 256:384], ones1_h.ap())
    dma(Y, c_cst, x_tm.ap()[:, 384:385], ones128_h.ap())
    vcopy(tri_b.ap(), x_tm.ap()[:, 0:128], waits=[(c_cst, None)], incs=[(c_dv, 1)])
    vcopy(idm_b.ap(), x_tm.ap()[:, 128:256], incs=[(c_dv, 1)])
    vcopy(idm_f.ap(), x_tm.ap()[:, 128:256], incs=[(c_dv, 1)])
    vcopy(ones1_b.ap(), x_tm.ap()[0:1, 256:384], incs=[(c_dv, 1)])
    vcopy(ones128_b.ap(), x_tm.ap()[:, 384:385], incs=[(c_dv, 1)])
    p.step(V, lambda e: e.memset(carry[0], 0.0), incs=[(c_dv, 1)])
    p.step(V, lambda e: e.memset(c_lneps.ap(), LN_EPS), incs=[(c_dv, 1)])
    p.step(V, lambda e: e.memset(c_rmseps.ap(), RMS_EPS), incs=[(c_dv, 1)])
    p.step(V, lambda e: e.memset(c_weps.ap(), W_EPS), incs=[(c_dv, 1)])
    p.step(V, lambda e: e.memset(c_i1024.ap(), 1.0 / 1024.0), incs=[(c_dv, 1)])
    p.step(V, lambda e: e.memset(c_neg1.ap(), -1.0), incs=[(c_dv, 1)])
    p.step(V, lambda e: e.memset(c_nhalf.ap(), -0.5), incs=[(c_dv, 1)])
    p.step(V, lambda e: e.memset(c_m1em5.ap(), -LN_EPS), incs=[(c_dv, 1)])
    p.step(V, lambda e: e.memset(c_1p.ap(), 1.0 + RMS_EPS), incs=[(c_dv, 1)])
    p.step(V, lambda e: e.memset(c_m25.ap(), -2.5), incs=[(c_dv, 1)])
    p.step(V, lambda e: e.memset(c_m35.ap(), -3.5), incs=[(c_dv, 1)])
    p.step(V, lambda e: e.memset(c_m50.ap(), -5.0), incs=[(c_dv, 1)])
    consts_dv = c_dv.n

    cast_tasks = []
    for (wh_, ws_, width_) in ((wqkvT_h, qkvT_s, E3), (wswT_h, swT_s, ESW),
                               (woutT_h, outT_s, D)):
        for ch_ in range(KD):
            cast_tasks.append((wh_, ws_, width_, ch_))
    zc_w = max(NJ * 1024, 2816)
    cregs = {"A": aliasA.ap()[:, 0:3072], "B": aliasB.ap()[:, 0:3072],
             "U": u1q.ap()[:, 0:2816]}
    cguard = {"A": [], "B": [], "U": []}
    cslotg = {"big": [], "small": []}
    crot = {"big": ["A", "B"], "small": ["U", "A", "B"], "i": 0, "j": 0}

    def emit_cast(task, big_slot, big_sem):
        wh_, ws_, width_, ch_ = task
        big = width_ > 2048
        slot, sem, sk = (big_slot, big_sem, "big") if big else             (vp32.ap()[:, 0:2048], c_wA, "small")
        dma(S, sem, slot[:, 0:width_], wh_.ap()[ch_ * 128:(ch_ + 1) * 128, :],
            waits=[(sem, sem.n)] + cslotg[sk])
        if big:
            rk = crot["big"][crot["i"] % 2]
            crot["i"] += 1
        else:
            rk = crot["small"][crot["j"] % 3]
            crot["j"] += 1
        cr = cregs[rk]
        vcopy(cr[:, 0:width_], slot[:, 0:width_],
              waits=[(sem, sem.n), (c_dv, None)] + cguard[rk], incs=[(c_dv, 1)])
        cslotg[sk] = [(c_dv, c_dv.n)]
        dma(S, c_fo, ws_.ap()[ch_ * 128:(ch_ + 1) * 128, :], cr[:, 0:width_],
            waits=[(c_fo, c_fo.n), (c_dv, None)])
        cguard[rk] = [(c_fo, c_fo.n)]

    interleave_casts = zc_w >= 3072
    if not interleave_casts:
        for task in cast_tasks:
            emit_cast(task, acc.ap()[:, 0:3072], c_cB)
        cast_tasks = []
    consts_g = [(c_dv, consts_dv)]

    gsl = [acc.ap()[:, 0:1024], acc.ap()[:, 1024:2048],
           x_tm.ap()[:, 0:1024], x_tm.ap()[:, 1024:2048]]
    gsem = [c_pA, c_pB, c_cA, c_cr]
    gguard = [list(cslotg["big"]), list(cslotg["big"]),
              consts_g + cslotg["big"], consts_g + cslotg["big"]]
    pro_sets = []
    ns4 = 4 if max(NJ * 1024, 2816) >= 4096 else 2
    for s4 in range(ns4):
        o = s4 * 1024
        pro_sets.append(dict(
            set_id=s4, ip=True,
            vp=gsl[s4],
            u1=aliasD.ap()[:, o:o + 1024],
            z1=arenaQ.ap()[:, o:o + 1024],
            q2=arenaQ.ap()[:, 4096 + o:4096 + o + 1024],
            sg=arena24.ap()[:, o:o + 1024],
            bm=acc.ap()[:, 2048 + s4 * 64:2048 + s4 * 64 + 64],
            sc=acc.ap()[:, 2048 + s4 * 64:2048 + s4 * 64 + 64],
            sh=acc.ap()[:, 2304 + s4 * 64:2304 + s4 * 64 + 64],
            rs=acc.ap()[:, 2560 + s4 * 64:2560 + s4 * 64 + 64],
            vp16=(scr1 if s4 < 2 else vpx).ap()[:, (s4 % 2) * 1024:(s4 % 2) * 1024 + 1024]))
    wqouts4 = [arena24.ap()[:, 4096 + s4 * 1024:4096 + s4 * 1024 + 1024] for s4 in range(4)]
    ttmp4 = [arena24.ap()[:, 8192 + s4 * 1024:8192 + s4 * 1024 + 1024] for s4 in range(4)]
    ldgs = {}
    pjob = 0

    NWV = 4 if ns4 == 4 else 2

    def _gu_wave(i0):
        jobs = []
        idxs4 = [i for i in range(i0, i0 + NWV)]
        for i in idxs4:
            sem_n, wsT, ch, sl = ldgs[i]
            jobs.append((gsl[sl], wqouts4[sl], 1024, pro_sets[sl % ns4]))
        quant_multi(jobs, waits=[ldgs[i][0] for i in idxs4] + [(c_fo, None), (c_cB, None)],
                    wave=NWV)
        qg = [(c_dv, c_dv.n), (c_pl, c_pl.n)]
        for i in idxs4:
            sem_n, wsT, ch, sl = ldgs[i]
            gguard[sl] = [(c_dv, c_dv.n), (c_ac, c_ac.n), (c_pl, c_pl.n)]
            transpose_group(
                [wqouts4[sl][:, fc * 128:(fc + 1) * 128] for fc in range(8)],
                ttmp4[sl].rearrange("p (k w) -> p k w", w=128),
                deps=qg + [(c_fo, None)])
            dst = wsT.ap()[:, ch * 128:(ch + 1) * 128].rearrange("(b p) w -> p b w", p=128)
            dma(S, c_fo, dst, ttmp4[sl].rearrange("p (b w) -> p b w", b=8),
                waits=[(c_fo, c_fo.n), (c_ac, None)])

    for (wh, wsT, nrows) in ((wg_h, wgT_s, DFF), (wu_h, wuT_s, DFF)):
        for ch in range(nrows // 128):
            sl = pjob % 4
            wts = [(gsem[sl], gsem[sl].n)] + list(gguard[sl])
            dma(S, gsem[sl], gsl[sl], wh.ap()[ch * 128:(ch + 1) * 128, :], waits=wts)
            ldgs[pjob] = ((gsem[sl], gsem[sl].n), wsT, ch, sl)
            pjob += 1
            if pjob % NWV == 0:
                _gu_wave(pjob - NWV)
                for _ in range(2):
                    if cast_tasks:
                        emit_cast(cast_tasks.pop(0), zc.ap()[:, 0:3072], c_cB)

    dslots = [acc.ap()[:, 0:2816], x_tm.ap()[:, 0:2816]]
    dsem = [c_pA, c_pB]
    dguard = [gguard[0] + gguard[1] + [(c_pl, None), (c_dv, None), (c_ac, None)],
              gguard[2] + gguard[3] + [(c_pl, None), (c_dv, None), (c_ac, None)]]
    down_sets = []
    for par in range(2):
        o = par * 1408
        down_sets.append(dict(
            set_id=par, ip=True,
            u1=u1q.ap()[:, o:o + 1408],
            z1=arenaQ.ap()[:, o:o + 1408],
            q2=arenaQ.ap()[:, 4096 + o:4096 + o + 1408],
            sg=arena24.ap()[:, o:o + 1408],
            bm=x_tm.ap()[:, 2864 + par * 88:2864 + par * 88 + 88],
            sc=x_tm.ap()[:, 2864 + par * 88:2864 + par * 88 + 88],
            sh=x_tm.ap()[:, 3040 + par * 88:3040 + par * 88 + 88],
            rs=x_tm.ap()[:, 3216 + par * 88:3216 + par * 88 + 88],
            vp16=(scr1 if par == 0 else vpx).ap()[:, 0:1408]))
    dwq = [arena24.ap()[:, 4096 + par * 1408:4096 + par * 1408 + 1408] for par in range(2)]
    ttmpd = arena24.ap()[:, 8192:8192 + 2816]
    for ch in range(D // 128):
        sl = ch % 2
        wts = [(dsem[sl], dsem[sl].n)] + list(dguard[sl])
        dma(S, dsem[sl], dslots[sl], wd_h.ap()[ch * 128:(ch + 1) * 128, :], waits=wts)
        ldg = (dsem[sl], dsem[sl].n)
        jobs = [(dslots[sl][:, hh * 1408:(hh + 1) * 1408], dwq[hh], 1408,
                 dict(down_sets[hh], vp=dslots[sl][:, hh * 1408:(hh + 1) * 1408]))
                for hh in range(2)]
        w0 = [ldg, (c_fo, None)]
        if ch == 0:
            w0 += [(c_dv, None), (c_ac, None), (c_pl, None), (c_pe, None)]
        quant_multi(jobs, waits=w0)
        dguard[sl] = [(c_dv, c_dv.n), (c_ac, c_ac.n), (c_pl, c_pl.n)]
        qg = [(c_dv, c_dv.n), (c_pl, c_pl.n)]
        nfc = DFF // 128
        for g in range((nfc + 7) // 8):
            f0, f1 = g * 8, min(g * 8 + 8, nfc)
            srcs = []
            for fc in range(f0, f1):
                hh = (fc * 128) // 1408
                off = fc * 128 - hh * 1408
                srcs.append(dwq[hh][:, off:off + 128])
            transpose_group(srcs, ttmpd[:, f0 * 128:f1 * 128].rearrange(
                "p (k w) -> p k w", w=128), deps=qg + [(c_fo, None)])
        dst = wdT_s.ap()[:, ch * 128:(ch + 1) * 128].rearrange("(b p) w -> p b w", p=128)
        dma(S, c_fo, dst, ttmpd[:, 0:nfc * 128].rearrange("p (b w) -> p b w", b=nfc),
            waits=[(c_fo, c_fo.n), (c_dv, None), (c_ac, None)])
        for _ in range(2):
            if cast_tasks:
                emit_cast(cast_tasks.pop(0), zc.ap()[:, 0:3072], c_cB)
    while cast_tasks:
        emit_cast(cast_tasks.pop(0), zc.ap()[:, 0:3072], c_cB)
    slot_guard = [None, (c_dv, c_dv.n)]
    slot_guard_pl = [None, (c_ac, c_ac.n)]
    pro_fo_g = (c_fo, c_fo.n)
    stages = [("qkv", qkvT_s, E3, KD), ("sw", swT_s, ESW, KD), ("out", outT_s, D, KD),
              ("gate", wgT_s, DFF, KD), ("up", wuT_s, DFF, KD), ("down", wdT_s, D, KF)]
    halves = []
    piece_map = {"qkv": (4, 4), "sw": (4, 4), "out": (4, 4),
                 "gate": (4, 4), "up": (4, 4), "down": (11, 11)}
    for t in range(NT):
        for (sname, ws, width, kc) in stages:
            k0 = 0
            for nk in piece_map[sname]:
                halves.append((ws, width, k0, nk))
                k0 += nk

    wstate = {"next": 0, "rel": [[pro_fo_g], [pro_fo_g]], "ld": {}}
    wsem = [c_wA, c_wB]

    def load_next_half():
        idx = wstate["next"]
        if idx >= len(halves):
            return
        ws, width, k0, nk = halves[idx]
        bi = idx % 2
        buf = wbufA if bi == 0 else wbufB
        wts = [(wsem[bi], wsem[bi].n)] + list(wstate["rel"][bi])
        if idx < 12 and ws in (wgT_s, wuT_s, wdT_s):
            wts.append(pro_fo_g)
        dma(S, wsem[bi], buf.ap()[:, 0:nk * width].rearrange("p (k w) -> p k w", w=width),
            ws.ap()[k0 * 128:(k0 + nk) * 128, :].rearrange("(k p) w -> p k w", p=128),
            waits=wts)
        wstate["ld"][idx] = (wsem[bi].n, buf, k0, nk, width)
        wstate["next"] = idx + 1

    def use_half(idx):
        ld, buf, k0, nk, width = wstate["ld"][idx]
        return (wsem[idx % 2], ld), buf.ap()[:, 0:nk * width].rearrange("p (k w) -> p k w", w=width), k0, nk

    def release_half(idx):
        wstate["rel"][idx % 2] = [(c_pe, c_pe.n)]
        load_next_half()

    load_next_half()
    load_next_half()
    hidx = [0]

    def stage_pieces(n):
        idxs = list(range(hidx[0], hidx[0] + n))
        hidx[0] += n
        infos = [use_half(i) for i in idxs]
        return idxs, infos

    def piece_for_k(infos, k):
        for gd, vw, k0, nk in infos:
            if k0 <= k < k0 + nk:
                return gd, vw, k0
        raise KeyError(k)

    def k_waits(infos, k, first_extra):
        wt = []
        if k == 0:
            wt += list(first_extra)
        for gd, vw, k0, nk in infos:
            if k == k0:
                wt.append(gd)
        return wt

    x_guard = [None]
    out_guard = [None]
    carry_ac_g = [None]
    sig_done = [None]
    kv_done = [None]

    def mm_stage(sname, lhs_fn, widths, evac_order, evac_fn, dep, KC):
        idxs, infos = stage_pieces(len(piece_map[sname]))
        ncf = len(widths)
        for j in range(NJ):
            banks = {}
            for ci_, c in enumerate(evac_order):
                n0, n1 = widths[c]
                pb, b, bw = next_bank(n1 - n0)
                banks[c] = (pb, b)
                for k in range(KC):
                    gd, wv, koff = piece_for_k(infos, k)
                    wt = list(bw) if k == 0 else []
                    if k == 0 and ci_ == 0:
                        wt.append(dep)
                    for gd2, _, k0_, _ in infos:
                        if k == k0_:
                            wt.append(gd2)
                    mm(pb, lhs_fn(k, j), wv[:, k - koff, n0:n1],
                       start=(k == 0), stop=(k == KC - 1), waits=wt,
                       incs=[(c_pe, 1)] if k == KC - 1 else ())
                pe_val = c_pe.n
                evac_fn(j, c, pb, b, pe_val)
        for i_ in idxs:
            release_half(i_)

    def emit_tile(t):
        jr = range(NJ)
        if t == 0:
            wx0 = [g for g in (slot_guard[1], slot_guard_pl[1]) if g is not None]
            dma(Y, c_x, x_v,
                x_h.ap()[0:TT, :].rearrange("(j p) d -> p j d", p=128),
                waits=wx0)
        xg = x_guard[0] if x_guard[0] is not None else (c_x, c_x.n)

        vreduce(s1r.ap(), x_v, A.add, waits=[xg, (c_dv, None)], incs=[(c_dv, 1)])
        vts(negm.ap(), s1r.ap(), -1.0 / 1024.0, None, A.mult,
            waits=[(c_dv, None)], incs=[(c_dv, 1)])
        negm_dv = c_dv.n
        junk = [scr1.ap()[:, 0:1024], scr1.ap()[:, 1024:2048],
                scr2.ap()[:, 0:1024], scr2.ap()[:, 1024:2048]]
        for j in jr:
            vts(zc_v[:, j, :], x_v[:, j, :], negm.ap()[:, j:j + 1], None, A.add,
                waits=[(c_dv, negm_dv)], incs=[(c_dv, 1)])
            act(AF.Square, junk[j % 4], x_v[:, j, :],
                bias=negm.ap()[:, j:j + 1], accum=s2r.ap()[:, j:j + 1],
                waits=[(c_dv, negm_dv), (c_ac, None)], incs=[(c_ac, 1)])
        zc_ac = c_ac.n
        zc_dv = c_dv.n
        act(AF.Ln, tmpr.ap(), s2r.ap(), bias=c_lneps.ap(), scale=c_i1024.ap(),
            waits=[(c_ac, None)], incs=[(c_ac, 1)])
        act(AF.Exp, r_r.ap(), tmpr.ap(), scale=c_nhalf.ap(),
            waits=[(c_ac, None)], incs=[(c_ac, 1)])
        act(AF.Square, tmpr2.ap(), r_r.ap(), waits=[(c_ac, None)], incs=[(c_ac, 1)])
        act(AF.Ln, tmpr.ap(), tmpr2.ap(), bias=c_1p.ap(), scale=c_m1em5.ap(),
            waits=[(c_ac, None)], incs=[(c_ac, 1)])
        act(AF.Exp, rr_r.ap(), tmpr.ap(), scale=c_nhalf.ap(),
            waits=[(c_ac, None)], incs=[(c_ac, 1)])
        rows_ac = c_ac.n
        vtt(alpha.ap(), r_r.ap(), rr_r.ap(), A.mult,
            waits=[(c_ac, rows_ac), (c_dv, None)], incs=[(c_dv, 1)])
        vts(gamma.ap(), r_r.ap(), 1.0, None, A.add, incs=[(c_dv, 1)])
        vtt(beta.ap(), negm.ap(), r_r.ap(), A.mult, incs=[(c_dv, 1)])
        rows_dv = c_dv.n
        for j in jr:
            wt = [(c_dv, rows_dv)]
            if out_guard[0] is not None:
                wt.append(out_guard[0])
            vts(acc_v[:, j, :], x_v[:, j, :],
                gamma.ap()[:, j:j + 1], beta.ap()[:, j:j + 1], A.mult, A.add,
                waits=wt, incs=[(c_dv, 1)])
        acc_dv = c_dv.n
        for j in jr:
            for g2 in range(2):
                pb, bnk, bw = next_bank(512)
                for fi in range(4):
                    fc = g2 * 4 + fi
                    wt = []
                    if fi == 0:
                        wt = [(c_dv, max(zc_dv, consts_dv))] + list(bw)
                    p.step(Tn, lambda e, o=pb[:, fi * 128:(fi + 1) * 128],
                           s_=zc_v[:, j, fc * 128:(fc + 1) * 128]:
                           e.transpose(o, s_, idm_f.ap()),
                           waits=wt, incs=[(c_pe, 1)] if fi == 3 else ())
                act(AF.Copy, fmD[:, g2 * 4:g2 * 4 + 4, j * 128:(j + 1) * 128],
                    pb.rearrange("p (k w) -> p k w", w=128)[:, 0:4, :],
                    waits=[(c_pe, None), (c_ac, None)], incs=[(c_ac, 1)])
                bank7[bnk] = (c_ac, c_ac.n)
        zcT_g = (c_ac, c_ac.n)

        qtmp = scr4.ap()[:, 1024:2048]
        ktmp = scr4.ap()[:, 0:1024]
        vtmp = scr3.ap()[:, 1024:2048]
        lnw = scr2.ap()[:, 0:1024]
        rinv = scr2.ap()[:, 1024:2048]
        idxs_q, infos_q = stage_pieces(len(piece_map["qkv"]))
        rot[0] = 0
        rot_mod[0] = 3
        yT_gs = []
        kv_vals = {}

        def qkv_grp(j):
            for c in range(6):
                pb, bq, bw = next_bank()
                for k in range(KD):
                    gd, wv, koff = piece_for_k(infos_q, k)
                    wt = list(bw) if k == 0 else []
                    if k == 0 and c == 0 and j == 0:
                        wt.append(zcT_g)
                    for gd2, _, k0_, _ in infos_q:
                        if k == k0_:
                            wt.append(gd2)
                    mm(pb, fmD[:, k, j * 128:(j + 1) * 128],
                       wv[:, k - koff, c * 512:(c + 1) * 512],
                       start=(k == 0), stop=(k == KD - 1),
                       waits=wt, incs=[(c_pe, 1)] if k == KD - 1 else ())
                pe_val = c_pe.n
                if c < 2:
                    dstp = qtmp[:, c * 512:(c + 1) * 512]
                elif c < 4:
                    dstp = ktmp[:, (c - 2) * 512:(c - 1) * 512]
                else:
                    dstp = vtmp[:, (c - 4) * 512:(c - 3) * 512]
                wt0 = [(c_pe, pe_val), (c_ac, None), (c_dv, rows_dv)]
                if c == 4:
                    wt0.append((c_cr, None))
                if c == 0 and sig_done[0] is not None:
                    wt0.append(sig_done[0])
                vts(dstp, pb, alpha.ap()[:, j:j + 1], None, A.mult,
                    waits=wt0, incs=[(c_dv, 1)])
                bank7[bq] = (c_dv, c_dv.n)
                if c == 1:
                    q_dv = c_dv.n
                elif c == 3:
                    k_dv = c_dv.n
            act(AF.Square, scr3.ap()[:, 0:1024], qtmp, accum=qs2.ap()[:, j:j + 1],
                waits=[(c_ac, None), (c_dv, q_dv), (c_cr, None)], incs=[(c_ac, 1)])
            act(AF.Square, scr3.ap()[:, 0:1024], ktmp, accum=ks2.ap()[:, j:j + 1],
                waits=[(c_ac, None), (c_dv, k_dv)], incs=[(c_ac, 1)])
            act(AF.Ln, tmpr.ap()[:, j:j + 1], qs2.ap()[:, j:j + 1],
                bias=c_rmseps.ap(), scale=c_i1024.ap(),
                waits=[(c_ac, None)], incs=[(c_ac, 1)])
            act(AF.Ln, tmpr2.ap()[:, j:j + 1], ks2.ap()[:, j:j + 1],
                bias=c_rmseps.ap(), scale=c_i1024.ap(), incs=[(c_ac, 1)])
            act(AF.Exp, rq_r.ap()[:, j:j + 1], tmpr.ap()[:, j:j + 1],
                scale=c_nhalf.ap(), waits=[(c_ac, None)], incs=[(c_ac, 1)])
            act(AF.Exp, rk_r.ap()[:, j:j + 1], tmpr2.ap()[:, j:j + 1],
                scale=c_nhalf.ap(), incs=[(c_ac, 1)])
            act(AF.Sigmoid, sigq[:, j, :], qtmp, scale=rq_r.ap()[:, j:j + 1],
                waits=[(c_ac, None), (c_dv, q_dv)], incs=[(c_ac, 1)])
            act(AF.Exp, wkv[:, j, 0:1024], ktmp, scale=rk_r.ap()[:, j:j + 1],
                waits=[(c_dv, k_dv)], incs=[(c_ac, 1)])
            vtt(wkv[:, j, 1024:2048], wkv[:, j, 0:1024], vtmp[:, 0:1024], A.mult,
                waits=[(c_ac, None), (c_dv, None)], incs=[(c_dv, 1)])
            sig_done[0] = (c_dv, c_dv.n)
            kv_vals[j] = c_dv.n

        def cumy_grp(j):
            gj = t * NJ + j
            cin = carry[gj % 2]
            cout = carry[(gj + 1) % 2]
            for q4 in range(4):
                pb = pall.ap()[:, q4 * 512:(q4 + 1) * 512]
                wt = []
                if bank7[q4] is not None:
                    wt.append(bank7[q4])
                    bank7[q4] = None
                if q4 == 0:
                    wt += [(c_dv, kv_vals[j])]
                    if carry_ac_g[0] is not None:
                        wt.append(carry_ac_g[0])
                mm(pb, tri_b.ap(), wkv[:, j, q4 * 512:(q4 + 1) * 512],
                   start=True, stop=False, waits=wt)
                mm(pb, ones1_b.ap(), cin[0:1, q4 * 512:(q4 + 1) * 512],
                   start=False, stop=True, incs=[(c_pe, 1)] if q4 == 3 else ())
            cums_pe = c_pe.n
            act(AF.Copy, scr3.ap()[0:64, 0:2048], pall.ap()[64:128, 0:2048],
                waits=[(c_pe, cums_pe), (c_ac, None), (c_dv, None)], incs=[(c_ac, 1)])
            dma(S, c_cr, cout, scr3.ap()[63:64, 0:2048],
                waits=[(c_cr, c_cr.n), (c_ac, None)])
            carry_ac_g[0] = (c_cr, c_cr.n)
            act(AF.Ln, lnw, pall.ap()[:, 0:1024], bias=c_weps.ap(),
                waits=[(c_ac, None)], incs=[(c_ac, 1)])
            vtt(y_v[:, j, :], sigq[:, j, :], pall.ap()[:, 1024:2048], A.mult,
                waits=[(c_pe, cums_pe), (c_dv, None)], incs=[(c_dv, 1)])
            act(AF.Exp, rinv, lnw, scale=c_neg1.ap(),
                waits=[(c_ac, None), (c_dv, None)], incs=[(c_ac, 1)])
            vtt(y_v[:, j, :], y_v[:, j, :], rinv, A.mult,
                waits=[(c_ac, None), (c_dv, None)], incs=[(c_dv, 1)])
            yev_dv = c_dv.n
            for q4 in range(4):
                bank7[q4] = (c_dv, yev_dv)
            transpose_group(
                [y_v[:, j, fc * 128:(fc + 1) * 128] for fc in range(KD)],
                fmD[:, :, j * 128:(j + 1) * 128],
                deps=[(c_dv, yev_dv)])
            yT_gs.append((c_ac, c_ac.n))

        sw_state = {}

        def sw_grp(j):
            if "pieces" not in sw_state:
                sw_state["pieces"] = stage_pieces(len(piece_map["sw"]))
            idxs_sw, infos_sw = sw_state["pieces"]
            sgt = scr1.ap()[:, (j % 2) * 1024:(j % 2) * 1024 + 1024]
            for ci_, c in enumerate((2, 3, 0, 1)):
                pb, bq, bw = next_bank()
                for k in range(KD):
                    gd, wv, koff = piece_for_k(infos_sw, k)
                    wt = list(bw) if k == 0 else []
                    if k == 0 and ci_ == 0:
                        wt.append(yT_gs[j])
                    for gd2, _, k0_, _ in infos_sw:
                        if k == k0_:
                            wt.append(gd2)
                    mm(pb, fmD[:, k, j * 128:(j + 1) * 128],
                       wv[:, k - koff, c * 512:(c + 1) * 512],
                       start=(k == 0), stop=(k == KD - 1), waits=wt,
                       incs=[(c_pe, 1)] if k == KD - 1 else ())
                if c >= 2:
                    act(AF.Silu, sgt[:, (c - 2) * 512:(c - 1) * 512], pb,
                        waits=[(c_pe, None), (c_ac, None), (c_dv, None)], incs=[(c_ac, 1)])
                    bank7[bq] = (c_ac, c_ac.n)
                else:
                    vtt(s_v[:, j, c * 512:(c + 1) * 512],
                        sgt[:, c * 512:(c + 1) * 512], pb, A.mult,
                        waits=[(c_pe, None), (c_ac, None), (c_dv, None)], incs=[(c_dv, 1)])
                    bank7[bq] = (c_dv, c_dv.n)
            transpose_group(
                [s_v[:, j, fc * 128:(fc + 1) * 128] for fc in range(KD)],
                fmD[:, :, j * 128:(j + 1) * 128],
                deps=[(c_dv, None)])

        if NJ == 4:
            qkv_grp(0)
            qkv_grp(1)
            cumy_grp(0)
            qkv_grp(2)
            cumy_grp(1)
            qkv_grp(3)
            cumy_grp(2)
            cumy_grp(3)
            for i_ in idxs_q:
                release_half(i_)
            sw_grp(0)
            sw_grp(1)
            sw_grp(2)
            sw_grp(3)
        else:
            for j in jr:
                qkv_grp(j)
            for j in jr:
                cumy_grp(j)
            for i_ in idxs_q:
                release_half(i_)
            for j in jr:
                sw_grp(j)
        rot[0] = 0
        rot_mod[0] = 6
        for i_ in sw_state["pieces"][0]:
            release_half(i_)
        sT_g = (c_ac, c_ac.n)

        xq_sets = []
        for par in range(2):
            o = par * 1024
            xq_sets.append(dict(
                set_id=par,
                vp=vp32.ap()[:, o:o + 1024],
                u1=scr1.ap()[:, o:o + 1024], z1=scr3.ap()[:, o:o + 1024],
                q2=scr2.ap()[:, o:o + 1024],
                sg=scr4.ap()[:, 1024:2048] if par == 0 else x_tm.ap()[:, 0:1024],
                bm=bmrow.ap()[:, par * 64:par * 64 + 64],
                sc=scrow.ap()[:, par * 64:par * 64 + 64],
                sh=shrow.ap()[:, par * 64:par * 64 + 64],
                rs=rsrow.ap()[:, par * 64:par * 64 + 64],
                vp16=vpx.ap()[:, o:o + 1024]))
        quant_multi([(zc_v[:, j2, :], xq_v[:, j2, :], 1024, xq_sets[j2 % 2])
                     for j2 in range(min(2, NJ))],
                    waits=[(c_ac, None), (c_dv, None), (c_cr, None),
                           (c_pl, None), (c_pe, None)])
        xq_g01 = [(c_dv, c_dv.n), (c_pl, c_pl.n)]

        def out_evac(j, c, pb, b, pe_val):
            vtt(acc_v[:, j, c * 512:(c + 1) * 512],
                acc_v[:, j, c * 512:(c + 1) * 512], pb, A.add,
                waits=[(c_pe, pe_val), (c_dv, acc_dv)], incs=[(c_dv, 1)])
            bank7[b] = (c_dv, c_dv.n)

        mm_stage("out", lambda k, j: fmD[:, k, j * 128:(j + 1) * 128],
                 [(0, 512), (512, 1024)], (0, 1), out_evac, sT_g, KD)
        o_dv = c_dv.n
        if NJ > 2:
            quant_multi([(zc_v[:, j2, :], xq_v[:, j2, :], 1024, xq_sets[j2 % 2])
                         for j2 in range(2, NJ)],
                        waits=[(c_ac, None)])
        xq_g23 = [(c_dv, c_dv.n), (c_pl, c_pl.n)]
        xqc_dv = c_dv.n
        xqc_pl = c_pl.n
        if t + 1 < NT:
            dma(Y, c_x, x_v,
                x_h.ap()[(t + 1) * TT:(t + 2) * TT, :].rearrange("(j p) d -> p j d", p=128),
                waits=[(c_x, c_x.n), (c_ac, None), (c_dv, xqc_dv), (c_pl, xqc_pl)])
            x_guard[0] = (c_x, c_x.n)

        NCF = (DFF + 511) // 512
        widthsF = [(c * 512, min((c + 1) * 512, DFF)) for c in range(NCF)]
        idxs_g, infos_g = stage_pieces(len(piece_map["gate"]))
        xqT_js = []
        for j in jr:
            transpose_group(
                [xq_v[:, j, fc * 128:(fc + 1) * 128] for fc in range(KD)],
                fmD[:, :, j * 128:(j + 1) * 128],
                deps=(xq_g01 if j < 2 else xq_g23) + [(c_dv, o_dv)])
            xqT_js.append((c_ac, c_ac.n))
            sgate = hq_v[:, j, :]
            for c in range(NCF):
                n0, n1 = widthsF[c]
                pb, bq, bw = next_bank(n1 - n0)
                for k in range(KD):
                    gd, wv, koff = piece_for_k(infos_g, k)
                    wt = list(bw) if k == 0 else []
                    if k == 0 and c == 0:
                        wt.append(xqT_js[j])
                    for gd2, _, k0_, _ in infos_g:
                        if k == k0_:
                            wt.append(gd2)
                    mm(pb, fmD[:, k, j * 128:(j + 1) * 128], wv[:, k - koff, n0:n1],
                       start=(k == 0), stop=(k == KD - 1), waits=wt,
                       incs=[(c_pe, 1)] if k == KD - 1 else ())
                act(AF.Silu, sgate[:, n0:n1], pb, scale=r_r.ap()[:, j:j + 1],
                    waits=[(c_pe, None), (c_ac, None), (c_dv, None)],
                    incs=[(c_ac, 1)])
                bank7[bq] = (c_ac, c_ac.n)
        for i_ in idxs_g:
            release_half(i_)
        xqT_g = (c_ac, c_ac.n)

        hq_sets = []
        for par in range(2):
            o = par * 1408
            hq_sets.append(dict(
                set_id=par,
                vp=zc.ap()[:, o:o + 1408],
                u1=u1q.ap()[:, o:o + 1408],
                z1=(scr2 if par == 0 else scr3).ap()[:, 0:1408],
                q2=aliasB.ap()[:, o:o + 1408], sg=aliasA.ap()[:, o:o + 1408],
                bm=bmrow.ap()[:, par * 88:par * 88 + 88],
                sc=scrow.ap()[:, par * 88:par * 88 + 88],
                sh=shrow.ap()[:, par * 88:par * 88 + 88],
                rs=rsrow.ap()[:, par * 88:par * 88 + 88],
                vp16=(scr1 if par == 0 else vpx).ap()[:, 0:1408]))
        idxs, infos = stage_pieces(len(piece_map["up"]))
        chain_gs = []
        for j in jr:
            sgate = hq_v[:, j, :]
            for c in range(NCF):
                n0, n1 = widthsF[c]
                pb, b, bw = next_bank(n1 - n0)
                for k in range(KD):
                    gd, wv, koff = piece_for_k(infos, k)
                    wt = list(bw) if k == 0 else []
                    if k == 0 and c == 0 and j == 0:
                        wt.append(xqT_g)
                    for gd2, _, k0_, _ in infos:
                        if k == k0_:
                            wt.append(gd2)
                    mm(pb, fmD[:, k, j * 128:(j + 1) * 128], wv[:, k - koff, n0:n1],
                       start=(k == 0), stop=(k == KD - 1), waits=wt,
                       incs=[(c_pe, 1)] if k == KD - 1 else ())
                vsts(h_v[:, j, n0:n1], pb, r_r.ap()[:, j:j + 1],
                     sgate[:, n0:n1], A.mult, A.mult,
                     waits=[(c_pe, None), (c_ac, None), (c_dv, None)],
                     incs=[(c_dv, 1)])
                bank7[b] = (c_dv, c_dv.n)
            hj_dv = c_dv.n
            quant_multi([(h_v[:, j, hh * 1408:(hh + 1) * 1408],
                          hq_v[:, j, hh * 1408:(hh + 1) * 1408],
                          1408, hq_sets[hh]) for hh in range(2)],
                        waits=[(c_dv, hj_dv), (c_ac, None)])
            chain_gs.append([(c_dv, c_dv.n), (c_pl, c_pl.n)])
        for i_ in idxs:
            release_half(i_)

        idxs, infos = stage_pieces(len(piece_map["down"]))
        for j in jr:
            for g in range(3):
                f0, f1 = g * 8, min(g * 8 + 8, KF)
                transpose_group(
                    [hq_v[:, j, fc * 128:(fc + 1) * 128] for fc in range(f0, f1)],
                    hqT2[:, j, f0:f1, :],
                    deps=chain_gs[j])
            hqT_j = (c_ac, c_ac.n)
            for c in range(2):
                pb, b, bw = next_bank()
                for k in range(KF):
                    gd, wv, koff = piece_for_k(infos, k)
                    wt = list(bw) if k == 0 else []
                    if k == 0:
                        wt.append(hqT_j)
                    for gd2, _, k0_, _ in infos:
                        if k == k0_:
                            wt.append(gd2)
                    mm(pb, hqT2[:, j, k, :], wv[:, k - koff, c * 512:(c + 1) * 512],
                       start=(k == 0), stop=(k == KF - 1), waits=wt,
                       incs=[(c_pe, 1)] if k == KF - 1 else ())
                wt2 = [(c_pe, None), (c_dv, o_dv)]
                if out_guard[0] is not None:
                    wt2.append(out_guard[0])
                vtt(acc_v[:, j, c * 512:(c + 1) * 512],
                    acc_v[:, j, c * 512:(c + 1) * 512], pb, A.add,
                    waits=wt2, incs=[(c_dv, 1)])
                bank7[b] = (c_dv, c_dv.n)
        for i_ in idxs:
            release_half(i_)
        dma(Y, c_out, out_h.ap()[t * TT:(t + 1) * TT, :].rearrange("(j p) d -> p j d", p=128),
            acc.ap()[:, 0:NJ * 1024].rearrange("p (j w) -> p j w", w=1024),
            waits=[(c_out, c_out.n), (c_dv, None)])
        out_guard[0] = (c_out, c_out.n)

    for t in range(NT):
        emit_tile(t)

    p.step(G, lambda e: e.nop(), waits=[(c_out, None)])

    with nc.Block() as block:
        @block.sync
        def _(e):
            for f in p.q["sync"]:
                f(e)

        @block.scalar
        def _(e):
            for f in p.q["scalar"]:
                f(e)

        @block.vector
        def _(e):
            for f in p.q["vector"]:
                f(e)

        @block.tensor
        def _(e):
            for f in p.q["tensor"]:
                f(e)

        @block.gpsimd
        def _(e):
            for f in p.q["gpsimd"]:
                f(e)

    es.close()
    return nc


_NC_CACHE = {}


def kernel(L0_attn_x, L0_attn_w_qkv, L0_attn_w_swiglu, L0_attn_w_out,
           L0_ffn_w_gate, L0_ffn_w_up, L0_ffn_w_down):
    x = np.ascontiguousarray(np.asarray(L0_attn_x, dtype=np.float32))
    B, T, _ = x.shape
    key = (T,)
    if key not in _NC_CACHE:
        _NC_CACHE[key] = build(TT=512, NT=T // 512)
    nc = _NC_CACHE[key]
    wqkvT = np.ascontiguousarray(np.asarray(L0_attn_w_qkv, dtype=np.float32).T)
    wswT = np.ascontiguousarray(np.asarray(L0_attn_w_swiglu, dtype=np.float32).T)
    woutT = np.ascontiguousarray(np.asarray(L0_attn_w_out, dtype=np.float32).T)
    wg = np.ascontiguousarray(np.asarray(L0_ffn_w_gate, dtype=np.float32))
    wu = np.ascontiguousarray(np.asarray(L0_ffn_w_up, dtype=np.float32))
    wd = np.ascontiguousarray(np.asarray(L0_ffn_w_down, dtype=np.float32))
    tri = np.triu(np.ones((128, 128), dtype=np.float32))
    idm = np.eye(128, dtype=np.float32)
    ones1 = np.ones((1, 128), dtype=np.float32)
    ones128 = np.ones((128, 1), dtype=np.float32)
    in_maps = []
    for b in range(B):
        in_maps.append({
            "x": np.ascontiguousarray(x[b]),
            "wqkvT": wqkvT, "wswT": wswT, "woutT": woutT,
            "wg": wg, "wu": wu, "wd": wd,
            "tri": tri, "idm": idm, "ones1": ones1, "ones128": ones128,
        })
    res = run_bass_kernel_spmd(nc, in_maps, core_ids=list(range(B)))
    out = np.stack([np.asarray(res.results[b]["out"]) for b in range(B)], axis=0)
    return out.astype(np.float32)
